# version 01
# baseline (speedup 1.0000x reference)
import math
from contextlib import ExitStack

import numpy as np
import ml_dtypes

import concourse.bass as bass
import concourse.mybir as mybir

F32 = mybir.dt.float32
BF16 = mybir.dt.bfloat16
AF = mybir.ActivationFunctionType
ALU = mybir.AluOpType
AX = mybir.AxisListType
bfdt = ml_dtypes.bfloat16

T = 2048
D = 1024


def cdiv(a, b):
    return (a + b - 1) // b


class Sem:
    __slots__ = ("h", "v")
    def __init__(self, h):
        self.h = h
        self.v = 0
    def inc(self, n=1):
        self.v += n
        return self.v


class MK:
    def __init__(self):
        self.nc = bass.Bass("TRN2", target_bir_lowering=False)
        self.es = ExitStack()
        self._consts = {}
        self._sems = {}
        self._nm = 0

    def name(self, p):
        self._nm += 1
        return f"{p}_{self._nm}"

    def sem(self, key):
        if key not in self._sems:
            self._sems[key] = Sem(self.es.enter_context(self.nc.semaphore(f"s{key}")))
        return self._sems[key]

    def sbuf(self, name, shape, dtype):
        return self.es.enter_context(self.nc.sbuf_tensor(self.name(name), shape, dtype))

    def psum(self, name, shape, dtype=F32):
        return self.es.enter_context(self.nc.psum_tensor(self.name(name), shape, dtype))

    def din(self, name, shape, dtype=F32):
        return self.nc.dram_tensor(name, list(shape), dtype, kind="ExternalInput")

    def dout(self, name, shape, dtype=F32):
        return self.nc.dram_tensor(name, list(shape), dtype, kind="ExternalOutput")

    def dscratch(self, name, shape, dtype=BF16):
        return self.nc.dram_tensor(self.name(name), list(shape), dtype, kind="Internal")

    def const(self, key, arr):
        if key not in self._consts:
            self._consts[key] = self.nc.inline_tensor(np.ascontiguousarray(arr), name=f"c_{key}")
        return self._consts[key]

    def ident_bf16_d(self):
        return self.const("identbf", np.eye(128, dtype=np.float32).astype(bfdt))

    def ones_f32_d(self):
        return self.const("ones128f", np.ones((128, 1), dtype=np.float32))

    def ones1_f32_d(self):
        return self.const("ones1x128f", np.ones((1, 128), dtype=np.float32))

    def rowsel_d(self, e, nrows=8):
        m = np.zeros((nrows, 128), dtype=np.float32)
        m[e, :] = 1.0
        return self.const(f"rsel{nrows}_{e}", m.astype(bfdt))

    def finish(self):
        self.es.close()
        return self.nc


def setup_common(mk: MK):
    nc = mk.nc
    mk.ident_s = mk.sbuf("ident", [128, 128], BF16)
    mk.identf_s = mk.sbuf("identf", [128, 128], F32)
    mk.ones1f_s = mk.sbuf("ones1f", [1, 128], F32)
    mk.onescol_s = mk.sbuf("onescf", [128, 1], F32)
    s = mk.sem("dma")
    with nc.Block() as b:
        @b.sync
        def _(e):
            e.dma_start(out=mk.ident_s[:, :], in_=mk.ident_bf16_d()[:, :]).then_inc(s.h, 16)
            e.dma_start(out=mk.identf_s[:, :], in_=mk.const("identf32", np.eye(128, dtype=np.float32))[:, :]).then_inc(s.h, 16)
            e.dma_start(out=mk.ones1f_s[:, :], in_=mk.ones1_f32_d()[:, :]).then_inc(s.h, 16)
            e.dma_start(out=mk.onescol_s[:, :], in_=mk.ones_f32_d()[:, :]).then_inc(s.h, 16)
            e.wait_ge(s.h, s.inc(64))


def prep_w(mk: MK, W, layout, Kdim, Mdim, row_off=0, col_off=0, tag="w"):
    nc = mk.nc
    out = mk.dscratch(f"p_{tag}", [Kdim, Mdim], BF16)
    s = mk.sem("dma")
    if layout == "io":
        kt = cdiv(Kdim, 128)
        with nc.sbuf_tensor(mk.name("stgf"), [128, Mdim], F32) as stgf, \
             nc.sbuf_tensor(mk.name("stgb"), [128, Mdim], BF16) as stgb:
            for k in range(kt):
                kp = min(128, Kdim - k * 128)
                with nc.Block() as b:
                    @b.sync
                    def _(e):
                        e.dma_start(out=stgf[0:kp, :],
                                    in_=W[row_off + k * 128: row_off + k * 128 + kp,
                                          col_off: col_off + Mdim]).then_inc(s.h, 16)
                        e.wait_ge(s.h, s.inc(16))
                with nc.Block() as b:
                    @b.vector
                    def _(e):
                        e.tensor_copy(out=stgb[0:kp, :], in_=stgf[0:kp, :])
                with nc.Block() as b:
                    @b.sync
                    def _(e):
                        e.dma_start(out=out[k * 128: k * 128 + kp, :],
                                    in_=stgb[0:kp, :]).then_inc(s.h, 16)
                        e.wait_ge(s.h, s.inc(16))
        return out
    assert layout == "oi"
    mt = cdiv(Mdim, 128)
    kt = cdiv(Kdim, 128)
    with nc.sbuf_tensor(mk.name("stgf"), [128, Kdim], F32) as stgf, \
         nc.sbuf_tensor(mk.name("stgb"), [128, Kdim], BF16) as stgb, \
         nc.sbuf_tensor(mk.name("stgt"), [128, cdiv(Kdim, 128) * 128], BF16) as stgt, \
         nc.psum_tensor(mk.name("ptr"), [128, 1024], BF16) as ptr:
        for m in range(mt):
            mp = min(128, Mdim - m * 128)
            with nc.Block() as b:
                @b.sync
                def _(e):
                    e.dma_start(out=stgf[0:mp, :],
                                in_=W[row_off + m * 128: row_off + m * 128 + mp,
                                      col_off: col_off + Kdim]).then_inc(s.h, 16)
                    e.wait_ge(s.h, s.inc(16))
            with nc.Block() as b:
                @b.vector
                def _(e):
                    e.tensor_copy(out=stgb[0:mp, :], in_=stgf[0:mp, :])
            for k0 in range(0, kt, 8):
                kn = min(8, kt - k0)
                with nc.Block() as b:
                    @b.tensor
                    def _(e):
                        for i in range(kn):
                            kp = min(128, Kdim - (k0 + i) * 128)
                            e.transpose(ptr[0:kp, i * 128:i * 128 + mp],
                                        stgb[0:mp, (k0 + i) * 128:(k0 + i) * 128 + kp],
                                        mk.ident_s[0:mp, 0:mp])
                with nc.Block() as b:
                    @b.vector
                    def _(e):
                        for i in range(kn):
                            kp = min(128, Kdim - (k0 + i) * 128)
                            e.tensor_copy(out=stgt[0:kp, (k0 + i) * 128:(k0 + i) * 128 + mp],
                                          in_=ptr[0:kp, i * 128:i * 128 + mp])
            with nc.Block() as b:
                @b.sync
                def _(e):
                    for k in range(kt):
                        kp = min(128, Kdim - k * 128)
                        e.dma_start(out=out[k * 128: k * 128 + kp, m * 128: m * 128 + mp],
                                    in_=stgt[0:kp, k * 128: k * 128 + mp]).then_inc(s.h, 16)
                        s.inc(16)
                    e.wait_ge(s.h, s.v)
    return out


def gemm(mk: MK, Wbf, Kdim, Mdim, x_tiles, stages, n_total=T, mgroup=2,
         nchunk=512, m_start=0, m_stop=None):
    nc = mk.nc
    kt = cdiv(Kdim, 128)
    mt = cdiv(Mdim, 128)
    if m_stop is None:
        m_stop = mt
    ncs = cdiv(n_total, nchunk)
    assert mgroup * ncs <= 8
    s = mk.sem("dma")
    with ExitStack() as les:
        ps = [les.enter_context(nc.psum_tensor(mk.name("gps"), [128, nchunk], F32))
              for _ in range(mgroup * ncs)]
        lhsT = les.enter_context(nc.sbuf_tensor(mk.name("glh"), [128, mgroup * kt * 128], BF16))
        for m0 in range(m_start, m_stop, mgroup):
            mcnt = min(mgroup, m_stop - m0)
            with nc.Block() as b:
                @b.sync
                def _(e):
                    for k in range(kt):
                        kp = min(128, Kdim - k * 128)
                        for mi in range(mcnt):
                            mp = min(128, Mdim - (m0 + mi) * 128)
                            e.dma_start(
                                out=lhsT[0:kp, (k * mgroup + mi) * 128:(k * mgroup + mi) * 128 + mp],
                                in_=Wbf[k * 128:k * 128 + kp,
                                        (m0 + mi) * 128:(m0 + mi) * 128 + mp]).then_inc(s.h, 16)
                            s.inc(16)
                    e.wait_ge(s.h, s.v)
            with nc.Block() as b:
                @b.tensor
                def _(e):
                    for mi in range(mcnt):
                        mp = min(128, Mdim - (m0 + mi) * 128)
                        for nci in range(ncs):
                            nw = min(nchunk, n_total - nci * nchunk)
                            for k in range(kt):
                                kp = min(128, Kdim - k * 128)
                                e.matmul(ps[mi * ncs + nci][0:mp, 0:nw],
                                         lhsT[0:kp, (k * mgroup + mi) * 128:(k * mgroup + mi) * 128 + mp],
                                         x_tiles(k)[0:kp, nci * nchunk: nci * nchunk + nw],
                                         start=(k == 0), stop=(k == kt - 1))
            for eng_name, fn in stages:
                with nc.Block() as b:
                    def body(e, fn=fn):
                        for mi in range(mcnt):
                            mp = min(128, Mdim - (m0 + mi) * 128)
                            for nci in range(ncs):
                                nw = min(nchunk, n_total - nci * nchunk)
                                fn(m0 + mi, nci, nw, ps[mi * ncs + nci][0:mp, 0:nw], e)
                    if eng_name == "scalar":
                        b.scalar(body)
                    elif eng_name == "vector":
                        b.vector(body)
                    else:
                        raise ValueError(eng_name)


def layernorm(mk: MK, x_tiles, out_tiles, eps=1e-5, gamma_col=None, beta_col=None,
              n_total=T, kt=8, dimD=None):
    nc = mk.nc
    if dimD is None:
        dimD = kt * 128
    inv_d = 1.0 / dimD
    with ExitStack() as les:
        ps_sum = les.enter_context(nc.psum_tensor(mk.name("lnps"), [1, 512], F32))
        ps_sq = les.enter_context(nc.psum_tensor(mk.name("lnpq"), [1, 512], F32))
        ps_arep = les.enter_context(nc.psum_tensor(mk.name("lnpa"), [128, 512], F32))
        ps_brep = les.enter_context(nc.psum_tensor(mk.name("lnpb"), [128, 512], F32))
        sq = les.enter_context(nc.sbuf_tensor(mk.name("lnsq"), [128, 512], F32))
        st = les.enter_context(nc.sbuf_tensor(mk.name("lnst"), [1, 512 * 4], F32))
        tmp = les.enter_context(nc.sbuf_tensor(mk.name("lntm"), [128, 512], F32))
        arep = les.enter_context(nc.sbuf_tensor(mk.name("lnar"), [128, 512], F32))
        brep = les.enter_context(nc.sbuf_tensor(mk.name("lnbr"), [128, 512], F32))
        for nci in range(cdiv(n_total, 512)):
            nw = min(512, n_total - nci * 512)
            sl = slice(nci * 512, nci * 512 + nw)
            with nc.Block() as b:
                @b.tensor
                def _(e):
                    for k in range(kt):
                        e.matmul(ps_sum[0:1, 0:nw], mk.onescol_s[:, :], x_tiles[k][:, sl],
                                 start=(k == 0), stop=(k == kt - 1))
            for k in range(kt):
                with nc.Block() as b:
                    @b.scalar
                    def _(e):
                        e.activation(out=sq[:, 0:nw], in_=x_tiles[k][:, sl], func=AF.Square)
                with nc.Block() as b:
                    @b.tensor
                    def _(e):
                        e.matmul(ps_sq[0:1, 0:nw], mk.onescol_s[:, :], sq[:, 0:nw],
                                 start=(k == 0), stop=(k == kt - 1))
            mu = st[0:1, 0:nw]
            var = st[0:1, 512:512 + nw]
            a = st[0:1, 1024:1024 + nw]
            bb = st[0:1, 1536:1536 + nw]
            with nc.Block() as b:
                @b.vector
                def _(e):
                    e.tensor_scalar_mul(out=mu, in0=ps_sum[0:1, 0:nw], scalar1=inv_d)
                    e.tensor_scalar_mul(out=var, in0=ps_sq[0:1, 0:nw], scalar1=inv_d)
                    e.drain()
                    e.tensor_mul(out=a, in0=mu, in1=mu)
                    e.drain()
                    e.tensor_sub(out=var, in0=var, in1=a)
                    e.drain()
                    e.tensor_scalar_add(out=var, in0=var, scalar1=float(eps))
                    e.drain()
            with nc.Block() as b:
                @b.scalar
                def _(e):
                    e.activation(out=a, in_=var, func=AF.Ln)
                    e.drain()
                    e.activation(out=a, in_=a, func=AF.Exp, scale=-0.5)
                    e.drain()
            with nc.Block() as b:
                @b.vector
                def _(e):
                    e.tensor_mul(out=bb, in0=mu, in1=a)
                    e.drain()
                    e.tensor_scalar_mul(out=bb, in0=bb, scalar1=-1.0)
                    e.drain()
            with nc.Block() as b:
                @b.tensor
                def _(e):
                    e.matmul(ps_arep[:, 0:nw], mk.ones1f_s[0:1, :], a, start=True, stop=True)
                    e.matmul(ps_brep[:, 0:nw], mk.ones1f_s[0:1, :], bb, start=True, stop=True)
            with nc.Block() as b:
                @b.vector
                def _(e):
                    e.tensor_copy(out=arep[:, 0:nw], in_=ps_arep[:, 0:nw])
                    e.tensor_copy(out=brep[:, 0:nw], in_=ps_brep[:, 0:nw])
            with nc.Block() as b:
                @b.vector
                def _(e):
                    for k in range(kt):
                        e.tensor_mul(out=tmp[:, 0:nw], in0=x_tiles[k][:, sl], in1=arep[:, 0:nw])
                        e.drain()
                        if gamma_col is None:
                            e.tensor_add(out=out_tiles[k][:, sl], in0=tmp[:, 0:nw], in1=brep[:, 0:nw])
                        else:
                            e.tensor_add(out=tmp[:, 0:nw], in0=tmp[:, 0:nw], in1=brep[:, 0:nw])
                            e.drain()
                            e.tensor_scalar(out=out_tiles[k][:, sl], in0=tmp[:, 0:nw],
                                            scalar1=gamma_col(k), scalar2=beta_col(k),
                                            op0=ALU.mult, op1=ALU.add)
                        e.drain()


def load_param_cols(mk: MK, vec_dram, dst, kt=8, off=0):
    nc = mk.nc
    s = mk.sem("dma")
    with nc.Block() as b:
        @b.sync
        def _(e):
            for k in range(kt):
                e.dma_start(out=dst[:, k:k + 1],
                            in_=vec_dram[off + k * 128: off + (k + 1) * 128].rearrange("(p o) -> p o", o=1)).then_inc(s.h, 16)
                s.inc(16)
            e.wait_ge(s.h, s.v)






def load_x(mk, x_in, x_t, kt=8):
    nc = mk.nc
    s = mk.sem("dma")
    with nc.Block() as b:
        @b.sync
        def _(e):
            for k in range(kt):
                e.dma_start(out=x_t[k][:, :], in_=x_in[k * 128:(k + 1) * 128, :]).then_inc(s.h, 16)
                s.inc(16)
            e.wait_ge(s.h, s.v)


def emit_transposes(mk, src_bf, dst_bf, ntiles, ptr):
    nc = mk.nc
    for t0 in range(0, ntiles, 8):
        tn = min(8, ntiles - t0)
        with nc.Block() as b:
            @b.tensor
            def _(e):
                for i in range(tn):
                    e.transpose(ptr[:, i * 128:(i + 1) * 128],
                                src_bf[:, (t0 + i) * 128:(t0 + i) * 128 + 128],
                                mk.ident_s[:, :])
        with nc.Block() as b:
            @b.vector
            def _(e):
                for i in range(tn):
                    e.tensor_copy(out=dst_bf[:, (t0 + i) * 128:(t0 + i) * 128 + 128],
                                  in_=ptr[:, i * 128:(i + 1) * 128])


def build_l0a():
    mk = MK()
    nc = mk.nc
    x_in = mk.din("x", (D, T))
    Wqkv = mk.din("Wqkv", (3072, 1024))
    kv_o = mk.dout("kv_p", (16, 64, 64))
    ks_o = mk.dout("ksum_p", (128, 8))
    setup_common(mk)
    x_t = [mk.sbuf("x", [128, T], F32) for _ in range(8)]
    xn_t = [mk.sbuf("xn", [128, T], BF16) for _ in range(8)]
    load_x(mk, x_in, x_t)
    layernorm(mk, x_t, xn_t)
    WT = prep_w(mk, Wqkv, "oi", 1024, 3072, tag="wqkv")

    kf_ex = mk.sbuf("kfex", [128, T], F32)
    kf_ln = mk.sbuf("kfln", [128, T], F32)
    kf_mk = mk.sbuf("kfmk", [128, T], mybir.dt.uint8)
    kb = mk.sbuf("kb", [128, T], BF16)
    vb = mk.sbuf("vb", [128, T], BF16)
    ktm = mk.sbuf("ktm", [128, T], BF16)
    vtm = mk.sbuf("vtm", [128, T], BF16)
    kscol = mk.sbuf("kscol", [128, 8], F32)
    kv_sb = mk.sbuf("kvsb", [64, 128], F32)
    ptr = mk.psum("ptr", [128, 1024], BF16)
    ps_kv = mk.psum("pskv", [64, 128], F32)

    def xt(k):
        return xn_t[k]

    s = mk.sem("dma")
    for p in range(8):
        def ev_sc(m, nci, nw, ps_ap, e):
            e.activation(out=kf_ex[:, nci * 512:nci * 512 + nw], in_=ps_ap, func=AF.Exp)
            e.activation(out=kf_ln[:, nci * 512:nci * 512 + nw], in_=ps_ap, func=AF.Identity, bias=1.0)
        def ev_ve(m, nci, nw, ps_ap, e):
            e.tensor_scalar(out=kf_mk[:, nci * 512:nci * 512 + nw], in0=ps_ap,
                            scalar1=0.0, scalar2=None, op0=ALU.is_gt)
        gemm(mk, WT, 1024, 3072, xt, [("scalar", ev_sc), ("vector", ev_ve)],
             mgroup=1, m_start=8 + p, m_stop=9 + p)
        with nc.Block() as b:
            @b.vector
            def _(e):
                e.select(out=kb[:, :], mask=kf_mk[:, :], on_true=kf_ln[:, :], on_false=kf_ex[:, :])
                e.reduce_sum(out=kscol[:, p:p + 1], in_=kb[:, :], axis=AX.X)
        def ev_v(m, nci, nw, ps_ap, e):
            e.copy(out=vb[:, nci * 512:nci * 512 + nw], in_=ps_ap)
        gemm(mk, WT, 1024, 3072, xt, [("scalar", ev_v)],
             mgroup=1, m_start=16 + p, m_stop=17 + p)
        emit_transposes(mk, kb, ktm, 16, ptr)
        emit_transposes(mk, vb, vtm, 16, ptr)
        with nc.Block() as b:
            @b.tensor
            def _(e):
                for tt in range(16):
                    e.matmul(ps_kv[0:64, 0:64],
                             ktm[:, tt * 128:tt * 128 + 64],
                             vtm[:, tt * 128:tt * 128 + 64],
                             start=(tt == 0), stop=(tt == 15))
                for tt in range(16):
                    e.matmul(ps_kv[0:64, 64:128],
                             ktm[:, tt * 128 + 64:tt * 128 + 128],
                             vtm[:, tt * 128 + 64:tt * 128 + 128],
                             start=(tt == 0), stop=(tt == 15))
        with nc.Block() as b:
            @b.vector
            def _(e):
                e.tensor_copy(out=kv_sb[:, :], in_=ps_kv[0:64, :])
        with nc.Block() as b:
            @b.sync
            def _(e):
                e.dma_start(out=kv_o[2 * p], in_=kv_sb[0:64, 0:64]).then_inc(s.h, 16)
                e.dma_start(out=kv_o[2 * p + 1], in_=kv_sb[0:64, 64:128]).then_inc(s.h, 16)
                e.wait_ge(s.h, s.inc(32))
    with nc.Block() as b:
        @b.sync
        def _(e):
            e.dma_start(out=ks_o[:, :], in_=kscol[:, :]).then_inc(s.h, 16)
            e.wait_ge(s.h, s.inc(16))
    return mk.finish()


def build_l0b():
    mk = MK()
    nc = mk.nc
    x_in = mk.din("x", (D, T))
    Wqkv = mk.din("Wqkv", (3072, 1024))
    Wproj = mk.din("Wproj", (1024, 1024))
    bproj = mk.din("bproj", (1024,))
    kv_all = mk.din("kv_all", (16, 4, 64, 64))
    ks_all = mk.din("ks_all", (8, 128, 4))
    rw = mk.din("router_w", (8, 1024))
    rb = mk.din("router_bias", (8,))
    sw1 = mk.din("sw1", (1024, 2048))
    sw3 = mk.din("sw3", (1024, 2048))
    sw2 = mk.din("sw2", (2048, 1024))
    ew1 = mk.din("ew1", (8, 1024, 1024))
    ew2 = mk.din("ew2", (8, 1024, 1024))
    x_out = mk.dout("x_new", (D, T))
    mk.s_dbg = mk.dout("s_dbg", (D, T), BF16)
    mk.g_dbg = mk.dout("g_dbg", (D, T))
    mk.comb_dbg = mk.dout("comb_dbg", (8, T), BF16)
    mk.shared_dbg = mk.dout("shared_dbg", (D, T))
    mk.tm_dbg = mk.dout("tm_dbg", (128, 128))

    setup_common(mk)
    x_t = [mk.sbuf("x", [128, T], F32) for _ in range(8)]
    xn_t = [mk.sbuf("xn", [128, T], BF16) for _ in range(8)]
    s_t = [mk.sbuf("s", [128, T], BF16) for _ in range(8)]
    load_x(mk, x_in, x_t)
    layernorm(mk, x_t, xn_t)

    WqT = prep_w(mk, Wqkv, "oi", 1024, 1024, row_off=0, tag="wq")
    WpT = prep_w(mk, Wproj, "oi", 1024, 1024, tag="wp")
    RwT = prep_w(mk, rw, "oi", 1024, 8, tag="rw")
    SW1 = prep_w(mk, sw1, "io", 1024, 2048, tag="sw1")
    SW3 = prep_w(mk, sw3, "io", 1024, 2048, tag="sw3")
    SW2 = prep_w(mk, sw2, "io", 2048, 1024, tag="sw2")
    EW1 = [prep_w(mk, ew1[e], "io", 1024, 1024, tag=f"ew1_{e}") for e in range(8)]
    EW2 = [prep_w(mk, ew2[e], "io", 1024, 1024, tag=f"ew2_{e}") for e in range(8)]

    def xt(k):
        return xn_t[k]

    s = mk.sem("dma")
    hm2 = mk.const("hm2", np.stack([(np.arange(128) < 64), (np.arange(128) >= 64)], 1)
                   .astype(np.float32).astype(bfdt))
    ps2 = mk.const("ps2", np.stack([(np.arange(128) < 64), (np.arange(128) >= 64)], 0)
                   .astype(np.float32).astype(bfdt))

    with ExitStack() as les:
        qf_ex = les.enter_context(nc.sbuf_tensor(mk.name("qfex"), [128, T], F32))
        qf_ln = les.enter_context(nc.sbuf_tensor(mk.name("qfln"), [128, T], F32))
        qf_mk = les.enter_context(nc.sbuf_tensor(mk.name("qfmk"), [128, T], mybir.dt.uint8))
        qs = les.enter_context(nc.sbuf_tensor(mk.name("qs"), [128, T], BF16))
        attm = [les.enter_context(nc.sbuf_tensor(mk.name("attm"), [128, T], BF16)) for _ in range(8)]
        qf = qf_ex
        tmpb = qs
        kvb = les.enter_context(nc.sbuf_tensor(mk.name("kvb"), [64, 4 * 64], F32))
        kvh = les.enter_context(nc.sbuf_tensor(mk.name("kvh"), [128, 64], BF16))
        ksl = les.enter_context(nc.sbuf_tensor(mk.name("ksl"), [128, 4], F32))
        ksc = les.enter_context(nc.sbuf_tensor(mk.name("ksc"), [128, 1], F32))
        den = les.enter_context(nc.sbuf_tensor(mk.name("den"), [2, T], F32))
        zb = les.enter_context(nc.sbuf_tensor(mk.name("zb"), [2, T], BF16))
        hm2_s = les.enter_context(nc.sbuf_tensor(mk.name("hm2s"), [128, 2], BF16))
        ps2_s = les.enter_context(nc.sbuf_tensor(mk.name("ps2s"), [2, 128], BF16))
        ps_den = les.enter_context(nc.psum_tensor(mk.name("pden"), [2, 512], F32))
        ps_rep = les.enter_context(nc.psum_tensor(mk.name("prep"), [128, 512], F32))
        ps_att = les.enter_context(nc.psum_tensor(mk.name("patt"), [128, 512], F32))
        with nc.Block() as b:
            @b.sync
            def _(e):
                e.dma_start(out=hm2_s[:, :], in_=hm2[:, :]).then_inc(s.h, 16)
                e.dma_start(out=ps2_s[:, :], in_=ps2[:, :]).then_inc(s.h, 16)
                e.wait_ge(s.h, s.inc(32))
        for p in range(8):
            def ev_sc(m, nci, nw, ps_ap, e):
                e.activation(out=qf_ex[:, nci * 512:nci * 512 + nw], in_=ps_ap, func=AF.Exp)
                e.activation(out=qf_ln[:, nci * 512:nci * 512 + nw], in_=ps_ap, func=AF.Identity, bias=1.0)
            def ev_ve(m, nci, nw, ps_ap, e):
                e.tensor_scalar(out=qf_mk[:, nci * 512:nci * 512 + nw], in0=ps_ap,
                                scalar1=0.0, scalar2=None, op0=ALU.is_gt)
            gemm(mk, WqT, 1024, 1024, xt, [("scalar", ev_sc), ("vector", ev_ve)],
                 mgroup=1, m_start=p, m_stop=p + 1)
            with nc.Block() as b:
                @b.sync
                def _(e):
                    e.dma_start(out=kvb[:, :].rearrange("d (i e) -> d i e", i=4),
                                in_=kv_all[2 * p].rearrange("i d e -> d i e")).then_inc(s.h, 16)
                    e.dma_start(out=ksl[:, :], in_=ks_all[p]).then_inc(s.h, 16)
                    e.wait_ge(s.h, s.inc(32))
            with nc.Block() as b:
                @b.vector
                def _(e):
                    e.select(out=qf[:, :], mask=qf_mk[:, :], on_true=qf_ln[:, :], on_false=qf_ex[:, :])
                    e.reduce_sum(out=ksc[:, 0:1], in_=ksl[:, :], axis=AX.X)
                    e.tensor_add(out=kvb[:, 0:64], in0=kvb[:, 0:64], in1=kvb[:, 64:128])
                    e.tensor_add(out=kvb[:, 128:192], in0=kvb[:, 128:192], in1=kvb[:, 192:256])
                    e.drain()
                    e.tensor_add(out=kvb[:, 0:64], in0=kvb[:, 0:64], in1=kvb[:, 128:192])
                    e.drain()
                    e.tensor_copy(out=kvh[0:64, :], in_=kvb[:, 0:64])
                    e.drain()
                    e.tensor_scalar(out=tmpb[:, :], in0=qf[:, :], scalar1=ksc[:, 0:1],
                                    scalar2=None, op0=ALU.mult)
            with nc.Block() as b:
                @b.sync
                def _(e):
                    e.dma_start(out=kvb[:, :].rearrange("d (i e) -> d i e", i=4),
                                in_=kv_all[2 * p + 1].rearrange("i d e -> d i e")).then_inc(s.h, 16)
                    e.wait_ge(s.h, s.inc(16))
            with nc.Block() as b:
                @b.vector
                def _(e):
                    e.tensor_add(out=kvb[:, 0:64], in0=kvb[:, 0:64], in1=kvb[:, 64:128])
                    e.tensor_add(out=kvb[:, 128:192], in0=kvb[:, 128:192], in1=kvb[:, 192:256])
                    e.drain()
                    e.tensor_add(out=kvb[:, 0:64], in0=kvb[:, 0:64], in1=kvb[:, 128:192])
                    e.drain()
                    e.tensor_copy(out=kvh[64:128, :], in_=kvb[:, 0:64])
            for nci in range(4):
                sl = slice(nci * 512, nci * 512 + 512)
                with nc.Block() as b:
                    @b.tensor
                    def _(e):
                        e.matmul(ps_den[:, :], hm2_s[:, :], tmpb[:, sl], start=True, stop=True)
                with nc.Block() as b:
                    @b.vector
                    def _(e):
                        e.tensor_scalar_add(out=den[:, sl], in0=ps_den[:, :], scalar1=1e-6)
                        e.drain()
                        e.reciprocal(out=den[:, sl], in_=den[:, sl])
                        e.drain()
                        e.tensor_copy(out=zb[:, sl], in_=den[:, sl])
                with nc.Block() as b:
                    @b.tensor
                    def _(e):
                        e.matmul(ps_rep[:, :], ps2_s[:, :], zb[:, sl], start=True, stop=True)
                with nc.Block() as b:
                    @b.vector
                    def _(e):
                        e.tensor_mul(out=qs[:, sl], in0=qf[:, sl], in1=ps_rep[:, :])
                with nc.Block() as b:
                    @b.tensor
                    def _(e):
                        e.matmul(ps_att[0:64, :], kvh[0:64, :], qs[0:64, sl], start=True, stop=True)
                        e.matmul(ps_att[64:128, :], kvh[64:128, :], qs[64:128, sl],
                                 start=True, stop=True, tile_position=(64, 64))
                with nc.Block() as b:
                    @b.vector
                    def _(e):
                        e.tensor_copy(out=attm[p][:, sl], in_=ps_att[:, :])
        bias_t = les.enter_context(nc.sbuf_tensor(mk.name("bias"), [128, 8], F32))
        load_param_cols(mk, bproj, bias_t, kt=8)
        def at(k):
            return attm[k]
        def ev_proj(m, nci, nw, ps_ap, e):
            e.activation(out=s_t[m][:, nci * 512:nci * 512 + nw], in_=ps_ap,
                         func=AF.Identity, bias=bias_t[:, m:m + 1])
        gemm(mk, WpT, 1024, 1024, at, [("scalar", ev_proj)], mgroup=1)
    with nc.Block() as b:
        @b.sync
        def _(e):
            for k in range(8):
                e.dma_start(out=mk.s_dbg[k * 128:(k + 1) * 128, :], in_=s_t[k][:, :]).then_inc(s.h, 16)
                s.inc(16)
            e.wait_ge(s.h, s.v)
    return mk, x_t, xn_t, s_t, (rb, RwT, SW1, SW3, SW2, EW1, EW2), x_out


def l0b_ffn_and_residual(mk, x_t, xn_t, s_t, ffn_handles, x_out):
    nc = mk.nc
    (rb, RwT, SW1, SW3, SW2, EW1, EW2) = ffn_handles
    s = mk.sem("dma")
    with ExitStack() as les:
        sb = lambda nm, shp, dt: les.enter_context(nc.sbuf_tensor(mk.name(nm), shp, dt))
        acc = [sb("acc", [128, 512], F32) for _ in range(8)]
        h1 = [sb("h1", [128, 512], BF16) for _ in range(16)]
        h3 = [sb("h3", [128, 512], BF16) for _ in range(16)]
        he = [sb("he", [128, 512], BF16) for _ in range(8)]
        comb_fm = sb("combfm", [8, 512], BF16)
        comb_rep = sb("combrep", [128, 512], BF16)
        probs = sb("probs", [128, 8 * 4], F32)
        sel = sb("sel", [128, 8 * 4], F32)
        oh1 = sb("oh1", [128, 8 * 4], F32)
        oh2 = sb("oh2", [128, 8 * 4], F32)
        csum = sb("csum", [128, 8 * 4], F32)
        onesf = sb("onesf", [128, 8], F32)
        colv = sb("colv", [128, 8 * 4], F32)
        rbias_rep = sb("rbrep", [128, 8], F32)
        rb_s = sb("rbs", [1, 8], F32)
        gtile = sb("gtile", [128, 512], F32)
        tmpf = sb("tmpf", [128, 512], F32)
        ps_r = les.enter_context(nc.psum_tensor(mk.name("psr"), [8, 512], F32))
        ps_t = les.enter_context(nc.psum_tensor(mk.name("pst2"), [128, 8 * 4], F32))
        ps_rep2 = les.enter_context(nc.psum_tensor(mk.name("prep2"), [128, 512], F32))
        ps_cm = les.enter_context(nc.psum_tensor(mk.name("pscm"), [8, 512], F32))
        with nc.Block() as b:
            @b.sync
            def _(e):
                e.dma_start(out=rb_s[:, :], in_=rb[:].rearrange("(o e) -> o e", o=1)).then_inc(s.h, 16)
                e.wait_ge(s.h, s.inc(16))
            @b.vector
            def _(e):
                e.memset(onesf[:, :], 1.0)
        with nc.Block() as b:
            @b.tensor
            def _(e):
                e.matmul(ps_rep2[:, 0:8], mk.ones1f_s[0:1, :], rb_s[0:1, :], start=True, stop=True)
        with nc.Block() as b:
            @b.vector
            def _(e):
                e.tensor_copy(out=rbias_rep[:, :], in_=ps_rep2[:, 0:8])
        rsel = [mk.rowsel_d(e_, 8) for e_ in range(8)]
        rsel_s = sb("rsels", [8, 8 * 128], BF16)
        with nc.Block() as b:
            @b.sync
            def _(e):
                for e_ in range(8):
                    e.dma_start(out=rsel_s[:, e_ * 128:(e_ + 1) * 128], in_=rsel[e_][:, :]).then_inc(s.h, 16)
                    s.inc(16)
                e.wait_ge(s.h, s.v)
        for nci in range(4):
            sl = slice(nci * 512, nci * 512 + 512)
            def xt_c(k):
                return xn_t[k][:, sl]
            def ev_r(m, nci2, nw, ps_ap, e):
                e.activation(out=probs[0:8, 0:512].rearrange("p n -> p n") if False else probs[0:8, 0:512],
                             in_=ps_ap, func=AF.Sigmoid)
            pass
        probs_fm = sb("probsfm", [8, 512], F32)
        for nci in range(4):
            sl = slice(nci * 512, nci * 512 + 512)
            def xt_c(k, sl=sl):
                return xn_t[k][:, sl]
            def ev_r(m, nci2, nw, ps_ap, e):
                e.activation(out=probs_fm[0:8, 0:nw], in_=ps_ap, func=AF.Sigmoid)
            gemm(mk, RwT, 1024, 8, xt_c, [("scalar", ev_r)], n_total=512, mgroup=1)
            with nc.Block() as b:
                @b.tensor
                def _(e):
                    for tt in range(4):
                        e.transpose(ps_t[:, tt * 8:(tt + 1) * 8],
                                    probs_fm[0:8, tt * 128:(tt + 1) * 128], mk.identf_s[0:8, 0:8])
            with nc.Block() as b:
                @b.vector
                def _(e):
                    for tt in range(4):
                        t8 = slice(tt * 8, (tt + 1) * 8)
                        e.tensor_copy(out=probs[:, t8], in_=ps_t[:, t8])
                        e.tensor_add(out=sel[:, t8], in0=ps_t[:, t8], in1=rbias_rep[:, 0:8])
            with nc.Block() as b:
                @b.vector
                def _(e):
                    def D():
                        e.drain()
                    for tt in range(4):
                        t8 = slice(tt * 8, (tt + 1) * 8)
                        cv = colv[:, tt * 8: tt * 8 + 1]
                        e.reduce_max(out=cv, in_=sel[:, t8], axis=AX.X)
                        D()
                        e.tensor_scalar(out=oh1[:, t8], in0=sel[:, t8], scalar1=cv,
                                        scalar2=None, op0=ALU.is_ge)
                        D()
                        e.tensor_tensor_scan(out=csum[:, t8], data0=onesf[:, :], data1=oh1[:, t8],
                                             initial=0.0, op0=ALU.mult, op1=ALU.add)
                        D()
                        e.tensor_sub(out=csum[:, t8], in0=csum[:, t8], in1=oh1[:, t8])
                        D()
                        e.tensor_scalar(out=csum[:, t8], in0=csum[:, t8], scalar1=0.5,
                                        scalar2=None, op0=ALU.is_lt)
                        D()
                        e.tensor_mul(out=oh1[:, t8], in0=oh1[:, t8], in1=csum[:, t8])
                        D()
                        e.tensor_scalar(out=csum[:, t8], in0=oh1[:, t8], scalar1=1e9,
                                        scalar2=None, op0=ALU.mult)
                        D()
                        e.tensor_sub(out=csum[:, t8], in0=sel[:, t8], in1=csum[:, t8])
                        D()
                        e.reduce_max(out=cv, in_=csum[:, t8], axis=AX.X)
                        D()
                        e.tensor_scalar(out=oh2[:, t8], in0=csum[:, t8], scalar1=cv,
                                        scalar2=None, op0=ALU.is_ge)
                        D()
                        e.tensor_tensor_scan(out=csum[:, t8], data0=onesf[:, :], data1=oh2[:, t8],
                                             initial=0.0, op0=ALU.mult, op1=ALU.add)
                        D()
                        e.tensor_sub(out=csum[:, t8], in0=csum[:, t8], in1=oh2[:, t8])
                        D()
                        e.tensor_scalar(out=csum[:, t8], in0=csum[:, t8], scalar1=0.5,
                                        scalar2=None, op0=ALU.is_lt)
                        D()
                        e.tensor_mul(out=oh2[:, t8], in0=oh2[:, t8], in1=csum[:, t8])
                        D()
            if nci == 0:
                with nc.Block() as b:
                    @b.sync
                    def _(e):
                        e.dma_start(out=mk.tm_dbg[:, 0:32], in_=probs[:, :]).then_inc(s.h, 16)
                        e.dma_start(out=mk.tm_dbg[:, 32:64], in_=sel[:, :]).then_inc(s.h, 16)
                        e.dma_start(out=mk.tm_dbg[:, 64:96], in_=oh1[:, :]).then_inc(s.h, 16)
                        e.dma_start(out=mk.tm_dbg[:, 96:128], in_=oh2[:, :]).then_inc(s.h, 16)
                        e.wait_ge(s.h, s.inc(64))
            with nc.Block() as b:
                @b.vector
                def _(e):
                    def D():
                        e.drain()
                    for tt in range(4):
                        t8 = slice(tt * 8, (tt + 1) * 8)
                        s1 = colv[:, tt * 8 + 1: tt * 8 + 2]
                        s2 = colv[:, tt * 8 + 2: tt * 8 + 3]
                        dn = colv[:, tt * 8 + 3: tt * 8 + 4]
                        dr = colv[:, tt * 8 + 4: tt * 8 + 5]
                        ok = colv[:, tt * 8 + 5: tt * 8 + 6]
                        e.tensor_mul(out=csum[:, t8], in0=oh1[:, t8], in1=probs[:, t8])
                        D()
                        e.reduce_sum(out=s1, in_=csum[:, t8], axis=AX.X)
                        D()
                        e.tensor_mul(out=csum[:, t8], in0=oh2[:, t8], in1=probs[:, t8])
                        D()
                        e.reduce_sum(out=s2, in_=csum[:, t8], axis=AX.X)
                        D()
                        e.tensor_add(out=dn, in0=s1, in1=s2)
                        D()
                        e.tensor_scalar_add(out=dr, in0=dn, scalar1=1e-9)
                        D()
                        e.reciprocal(out=dr, in_=dr)
                        D()
                        e.tensor_scalar(out=ok, in0=dn, scalar1=1e-9, scalar2=None, op0=ALU.is_gt)
                        D()
                        e.tensor_mul(out=s1, in0=s1, in1=dr)
                        D()
                        e.tensor_mul(out=s2, in0=s2, in1=dr)
                        D()
                        e.tensor_scalar_add(out=s1, in0=s1, scalar1=-0.5)
                        D()
                        e.tensor_mul(out=s1, in0=s1, in1=ok)
                        D()
                        e.tensor_scalar_add(out=s1, in0=s1, scalar1=0.5)
                        D()
                        e.tensor_scalar_add(out=s2, in0=s2, scalar1=-0.5)
                        D()
                        e.tensor_mul(out=s2, in0=s2, in1=ok)
                        D()
                        e.tensor_scalar_add(out=s2, in0=s2, scalar1=0.5)
                        D()
                        e.tensor_scalar(out=csum[:, t8], in0=oh1[:, t8], scalar1=s1,
                                        scalar2=None, op0=ALU.mult)
                        D()
                        e.tensor_scalar(out=oh2[:, t8], in0=oh2[:, t8], scalar1=s2,
                                        scalar2=None, op0=ALU.mult)
                        D()
                        e.tensor_add(out=oh1[:, t8], in0=csum[:, t8], in1=oh2[:, t8])
                        D()
            with nc.Block() as b:
                @b.tensor
                def _(e):
                    for tt in range(4):
                        e.transpose(ps_cm[0:8, tt * 128:(tt + 1) * 128],
                                    oh1[:, tt * 8:(tt + 1) * 8], mk.identf_s[:, :])
            with nc.Block() as b:
                @b.vector
                def _(e):
                    e.tensor_copy(out=comb_fm[:, :], in_=ps_cm[0:8, 0:512])
                @b.sync
                def _(e):
                    pass
            with nc.Block() as b:
                @b.sync
                def _(e):
                    e.dma_start(out=mk.comb_dbg[:, sl], in_=comb_fm[:, :]).then_inc(s.h, 16)
                    e.wait_ge(s.h, s.inc(16))
            def ev_h1(m, nci2, nw, ps_ap, e):
                e.activation(out=h1[m][:, 0:nw], in_=ps_ap, func=AF.Silu)
            gemm(mk, SW1, 1024, 2048, xt_c, [("scalar", ev_h1)], n_total=512, mgroup=1)
            def ev_h3(m, nci2, nw, ps_ap, e):
                e.copy(out=h3[m][:, 0:nw], in_=ps_ap)
            gemm(mk, SW3, 1024, 2048, xt_c, [("scalar", ev_h3)], n_total=512, mgroup=1)
            with nc.Block() as b:
                @b.vector
                def _(e):
                    for m in range(16):
                        e.tensor_mul(out=h1[m][:, :], in0=h1[m][:, :], in1=h3[m][:, :])
            def ev_sh(m, nci2, nw, ps_ap, e):
                e.tensor_copy(out=acc[m][:, 0:nw], in_=ps_ap)
            gemm(mk, SW2, 2048, 1024, lambda k: h1[k][:, :], [("vector", ev_sh)], n_total=512, mgroup=1)
            with nc.Block() as b:
                @b.sync
                def _(e):
                    for m in range(8):
                        e.dma_start(out=mk.shared_dbg[m * 128:(m + 1) * 128, sl], in_=acc[m][:, :]).then_inc(s.h, 16)
                        s.inc(16)
                    e.wait_ge(s.h, s.v)
            for e_ in range(8):
                with nc.Block() as b:
                    @b.tensor
                    def _(e):
                        e.matmul(ps_rep2[:, :], rsel_s[:, e_ * 128:(e_ + 1) * 128],
                                 comb_fm[0:8, :], start=True, stop=True)
                with nc.Block() as b:
                    @b.vector
                    def _(e):
                        e.tensor_copy(out=comb_rep[:, :], in_=ps_rep2[:, :])
                def ev_he(m, nci2, nw, ps_ap, e):
                    e.activation(out=he[m][:, 0:nw], in_=ps_ap, func=AF.Gelu)
                gemm(mk, EW1[e_], 1024, 1024, xt_c, [("scalar", ev_he)], n_total=512, mgroup=1)
                with nc.Block() as b:
                    @b.vector
                    def _(e):
                        for m in range(8):
                            e.tensor_mul(out=he[m][:, :], in0=he[m][:, :], in1=comb_rep[:, :])
                def ev_acc(m, nci2, nw, ps_ap, e):
                    e.tensor_add(out=acc[m][:, 0:nw], in0=acc[m][:, 0:nw], in1=ps_ap)
                gemm(mk, EW2[e_], 1024, 1024, lambda k: he[k][:, :], [("vector", ev_acc)], n_total=512, mgroup=1)
            with nc.Block() as b:
                @b.scalar
                def _(e):
                    for m in range(8):
                        e.activation(out=acc[m][:, :], in_=acc[m][:, :], func=AF.Silu)
            with nc.Block() as b:
                @b.vector
                def _(e):
                    for m in range(8):
                        e.tensor_copy(out=gtile[:, :], in_=s_t[m][:, sl])
                        e.drain()
                        e.tensor_mul(out=tmpf[:, :], in0=acc[m][:, :], in1=gtile[:, :])
                        e.drain()
                        e.tensor_add(out=x_t[m][:, sl], in0=x_t[m][:, sl], in1=tmpf[:, :])
            with nc.Block() as b:
                @b.sync
                def _(e):
                    for m in range(8):
                        e.dma_start(out=mk.g_dbg[m * 128:(m + 1) * 128, sl], in_=acc[m][:, :]).then_inc(s.h, 16)
                        s.inc(16)
                    e.wait_ge(s.h, s.v)
    with nc.Block() as b:
        @b.sync
        def _(e):
            for k in range(8):
                e.dma_start(out=x_out[k * 128:(k + 1) * 128, :], in_=x_t[k][:, :]).then_inc(s.h, 16)
                s.inc(16)
            e.wait_ge(s.h, s.v)


def build_l0b_full():
    mk, x_t, xn_t, s_t, ffn_handles, x_out = build_l0b()
    l0b_ffn_and_residual(mk, x_t, xn_t, s_t, ffn_handles, x_out)
    return mk.finish()






def softplus_col(mk, e, src_col, dst_col, t1, t2):
    e.activation(out=t1, in_=src_col, func=AF.Abs)
    e.drain()
    e.activation(out=t1, in_=t1, func=AF.Exp, scale=-1.0)
    e.drain()
    e.activation(out=t1, in_=t1, func=AF.Ln, bias=1.0)
    e.drain()


def build_l1_common(mk, with_carry, pre_h=None):
    nc = mk.nc
    x_in = mk.din("x", (D, T))
    W_in = mk.din("W_in", (1024, 1024))
    b_in = mk.din("b_in", (1024,))
    log_A = mk.din("log_A", (16, 64))
    B_p = mk.din("B_p", (16, 64))
    rw = mk.din("router_w", (8, 1024))
    setup_common(mk)
    x_t = [mk.sbuf("x", [128, T], F32) for _ in range(8)]
    xn_t = [mk.sbuf("xn", [128, T], BF16) for _ in range(8)]
    load_x(mk, x_in, x_t)
    layernorm(mk, x_t, xn_t)
    WinT = prep_w(mk, W_in, "oi", 1024, 1024, tag="win")
    RwT = prep_w(mk, rw, "oi", 1024, 8, tag="rw")

    bin_c = mk.sbuf("binc", [128, 8], F32)
    la_c = mk.sbuf("lac", [128, 8], F32)
    bp_c = mk.sbuf("bpc", [128, 8], F32)
    dec_c = mk.sbuf("decc", [128, 8], F32)
    t1 = mk.sbuf("t1", [128, 8], F32)
    load_param_cols(mk, b_in, bin_c, kt=8)
    load_param_cols(mk, log_A.ap().rearrange("a b -> (a b)"), la_c, kt=8)
    load_param_cols(mk, B_p.ap().rearrange("a b -> (a b)"), bp_c, kt=8)
    s = mk.sem("dma")
    with nc.Block() as b:
        @b.scalar
        def _(e):
            e.activation(out=t1[:, :], in_=la_c[:, :], func=AF.Abs)
            e.drain()
            e.activation(out=t1[:, :], in_=t1[:, :], func=AF.Exp, scale=-1.0)
            e.drain()
            e.activation(out=t1[:, :], in_=t1[:, :], func=AF.Ln, bias=1.0)
            e.drain()
    with nc.Block() as b:
        @b.vector
        def _(e):
            e.tensor_scalar_max(out=dec_c[:, :], in0=la_c[:, :], scalar1=0.0)
            e.drain()
            e.tensor_add(out=dec_c[:, :], in0=dec_c[:, :], in1=t1[:, :])
            e.drain()
    with nc.Block() as b:
        @b.scalar
        def _(e):
            e.activation(out=dec_c[:, :], in_=dec_c[:, :], func=AF.Exp, scale=-1.0)
    atile = mk.sbuf("atile", [128, 512], F32)
    carry = mk.sbuf("carry", [128, 8], F32)
    if pre_h is not None:
        pre_h()
    mk.h_es = ExitStack()
    h_t = [mk.h_es.enter_context(nc.sbuf_tensor(mk.name("h"), [128, T], F32)) for _ in range(8)]

    def ev_h(m, nci, nw, ps_ap, e):
        e.activation(out=h_t[m][:, nci * 512:nci * 512 + nw], in_=ps_ap,
                     func=AF.Identity, bias=bin_c[:, m:m + 1])
    gemm(mk, WinT, 1024, 1024, lambda k: xn_t[k], [("scalar", ev_h)])

    if with_carry:
        carry_in = mk.din("carry", (128, 8))
        with nc.Block() as b:
            @b.sync
            def _(e):
                e.dma_start(out=carry[:, :], in_=carry_in[:, :]).then_inc(s.h, 16)
                e.wait_ge(s.h, s.inc(16))
    else:
        with nc.Block() as b:
            @b.vector
            def _(e):
                e.memset(carry[:, :], 0.0)
    with nc.Block() as b:
        @b.vector
        def _(e):
            for k in range(8):
                e.tensor_scalar(out=h_t[k][:, :], in0=h_t[k][:, :], scalar1=bp_c[:, k:k + 1],
                                scalar2=None, op0=ALU.mult)
            e.drain()
    with nc.Block() as b:
        @b.vector
        def _(e):
            e.memset(atile[:, :], 1.0)
            e.drain()
    for k in range(8):
        with nc.Block() as b:
            @b.vector
            def _(e):
                e.memset(atile[:, :], 1.0)
                e.drain()
                e.tensor_scalar(out=atile[:, :], in0=atile[:, :], scalar1=dec_c[:, k:k + 1],
                                scalar2=None, op0=ALU.mult)
                e.drain()
                for c in range(4):
                    init = carry[:, k:k + 1] if c == 0 else h_t[k][:, c * 512 - 1:c * 512]
                    e.tensor_tensor_scan(out=h_t[k][:, c * 512:(c + 1) * 512],
                                         data0=atile[:, :],
                                         data1=h_t[k][:, c * 512:(c + 1) * 512],
                                         initial=init, op0=ALU.mult, op1=ALU.add)
                    e.drain()
    return x_t, xn_t, h_t, dec_c, RwT, s


def emit_counts(mk, RwT, xn_t, counts_out, s):
    nc = mk.nc
    with ExitStack() as les:
        sb = lambda nm, shp, dt: les.enter_context(nc.sbuf_tensor(mk.name(nm), shp, dt))
        lg_fm = sb("lgfm", [8, T], F32)
        ohb = sb("ohb", [128, 8], BF16)
        oh = sb("oh", [128, 8], F32)
        csum = sb("csum", [128, 8], F32)
        onesf = sb("onesf", [128, 8], F32)
        cv = sb("cv", [128, 1], F32)
        cnt_s = sb("cnts", [1, 8], F32)
        onescol_b = sb("onescb", [128, 1], BF16)
        ps_t = les.enter_context(nc.psum_tensor(mk.name("pst"), [128, 8], F32))
        ps_c = les.enter_context(nc.psum_tensor(mk.name("psc"), [1, 8], F32))
        with nc.Block() as b:
            @b.vector
            def _(e):
                e.memset(onesf[:, :], 1.0)
                e.memset(onescol_b[:, :], 1.0)
        for tt in range(16):
            with nc.Block() as b:
                @b.tensor
                def _(e):
                    e.transpose(ps_t[:, 0:8], lg_fm[0:8, tt * 128:(tt + 1) * 128],
                                mk.identf_s[0:8, 0:8])
            with nc.Block() as b:
                @b.vector
                def _(e):
                    e.reduce_max(out=cv[:, 0:1], in_=ps_t[:, 0:8], axis=AX.X)
                    e.drain()
                    e.tensor_scalar(out=oh[:, :], in0=ps_t[:, 0:8], scalar1=cv[:, 0:1],
                                    scalar2=None, op0=ALU.is_ge)
                    e.drain()
                    e.tensor_tensor_scan(out=csum[:, :], data0=onesf[:, :], data1=oh[:, :],
                                         initial=0.0, op0=ALU.mult, op1=ALU.add)
                    e.drain()
                    e.tensor_sub(out=csum[:, :], in0=csum[:, :], in1=oh[:, :])
                    e.drain()
                    e.tensor_scalar(out=csum[:, :], in0=csum[:, :], scalar1=0.5,
                                    scalar2=None, op0=ALU.is_lt)
                    e.drain()
                    e.tensor_mul(out=oh[:, :], in0=oh[:, :], in1=csum[:, :])
                    e.drain()
                    e.tensor_copy(out=ohb[:, :], in_=oh[:, :])
                    e.drain()
            with nc.Block() as b:
                @b.tensor
                def _(e):
                    e.matmul(ps_c[0:1, 0:8], onescol_b[:, 0:1], ohb[:, :],
                             start=(tt == 0), stop=(tt == 15))
        with nc.Block() as b:
            @b.vector
            def _(e):
                e.tensor_copy(out=cnt_s[0:1, :], in_=ps_c[0:1, 0:8])
        with nc.Block() as b:
            @b.sync
            def _(e):
                e.dma_start(out=counts_out[:, :], in_=cnt_s[0:1, :]).then_inc(s.h, 16)
                e.wait_ge(s.h, s.inc(16))


def build_l1a():
    mk = MK()
    nc = mk.nc
    state_o = mk.dout("state", (128, 8))
    counts_o = mk.dout("counts", (1, 8))
    pre = {}
    def pre_h():
        pre["st_c"] = mk.sbuf("stc", [128, 8], F32)
    x_t, xn_t, h_t, dec_c, RwT, s = build_l1_common(mk, with_carry=False, pre_h=pre_h)
    st_c = pre["st_c"]
    with nc.Block() as b:
        @b.vector
        def _(e):
            for k in range(8):
                e.tensor_copy(out=st_c[:, k:k + 1], in_=h_t[k][:, T - 1:T])
    with nc.Block() as b:
        @b.sync
        def _(e):
            e.dma_start(out=state_o[:, :], in_=st_c[:, :]).then_inc(s.h, 16)
            e.wait_ge(s.h, s.inc(16))
    emit_counts(mk, RwT, xn_t, counts_o, s)
    mk.h_es.close()
    return mk.finish()


def build_l1b():
    mk = MK()
    nc = mk.nc
    C_p = mk.din("C_p", (16, 64))
    W_out = mk.din("W_out", (1024, 1024))
    b_out = mk.din("b_out", (1024,))
    gamma = mk.din("gamma", (1024,))
    beta = mk.din("beta", (1024,))
    w1 = mk.din("w1", (8, 1024, 4096))
    b1 = mk.din("b1", (8, 4096))
    w2 = mk.din("w2", (8, 4096, 1024))
    b2 = mk.din("b2", (8, 1024))
    counts_all = mk.din("counts_all", (8, 8))
    maskrow = mk.din("maskrow", (8, 1))
    x_out = mk.dout("x_new", (D, T))

    pre = {}
    def pre_h():
        pre["WoT"] = prep_w(mk, W_out, "oi", 1024, 1024, tag="wo")
        pre["W1"] = [prep_w(mk, w1[e], "io", 1024, 4096, tag=f"w1_{e}") for e in range(8)]
        pre["W2"] = [prep_w(mk, w2[e], "io", 4096, 1024, tag=f"w2_{e}") for e in range(8)]
        pre["ybf"] = [mk.sbuf("ybf", [128, T], BF16) for _ in range(8)]
        for nm in ("cpc", "boc", "gac", "bec"):
            pre[nm] = mk.sbuf(nm, [128, 8], F32)

    x_t, xn_t, h_t, dec_c, RwT, s = build_l1_common(mk, with_carry=True, pre_h=pre_h)
    ybf = pre["ybf"]
    cp_c, bo_c, ga_c, be_c = pre["cpc"], pre["boc"], pre["gac"], pre["bec"]
    load_param_cols(mk, C_p.ap().rearrange("a b -> (a b)"), cp_c, kt=8)
    load_param_cols(mk, b_out, bo_c, kt=8)
    load_param_cols(mk, gamma, ga_c, kt=8)
    load_param_cols(mk, beta, be_c, kt=8)
    WoT, W1, W2 = pre["WoT"], pre["W1"], pre["W2"]

    with nc.Block() as b:
        @b.vector
        def _(e):
            for k in range(8):
                e.tensor_scalar(out=ybf[k][:, :], in0=h_t[k][:, :], scalar1=cp_c[:, k:k + 1],
                                scalar2=None, op0=ALU.mult)
    mk.h_es.close()
    s_t = [mk.sbuf("satt", [128, T], BF16) for _ in range(8)]

    with ExitStack() as les:
        out_c = [les.enter_context(nc.sbuf_tensor(mk.name("outc"), [128, 512], F32))
                 for _ in range(8)]
        for nci in range(4):
            sl = slice(nci * 512, nci * 512 + 512)
            def yb_c(k, sl=sl):
                return ybf[k][:, sl]
            def ev_o(m, nci2, nw, ps_ap, e):
                e.activation(out=out_c[m][:, 0:nw], in_=ps_ap,
                             func=AF.Identity, bias=bo_c[:, m:m + 1])
            gemm(mk, WoT, 1024, 1024, yb_c, [("scalar", ev_o)], n_total=512, mgroup=1)
            layernorm(mk, out_c, [s_t[k][:, sl] for k in range(8)], n_total=512,
                      gamma_col=lambda k: ga_c[:, k:k + 1], beta_col=lambda k: be_c[:, k:k + 1])

    CAP = 3072
    with ExitStack() as les:
        sb = lambda nm, shp, dt: les.enter_context(nc.sbuf_tensor(mk.name(nm), shp, dt))
        lg_fm = sb("lgfm", [8, T], F32)
        cnt_s = sb("cnta", [8, 8], F32)
        mrow = sb("mrow", [8, 1], F32)
        prefix = sb("prefix", [1, 8], F32)
        comb_fm = sb("combfm", [8, T], BF16)
        oh = sb("oh", [128, 8], F32)
        ohb = sb("ohb", [128, 8], BF16)
        csum = sb("csum", [128, 8], F32)
        onesf = sb("onesf", [128, 8], F32)
        cv = sb("cvl", [128, 4], F32)
        trib = sb("trib", [128, 128], BF16)
        acc = [sb("acc", [128, 512], F32) for _ in range(8)]
        he = [ybf[m // 4][:, (m % 4) * 512:(m % 4 + 1) * 512] for m in range(32)]
        comb_rep = sb("combrep", [128, 512], BF16)
        b1c = sb("b1c", [128, 32], F32)
        b2c = sb("b2c", [128, 8], F32)
        tmpf = sb("tmpfb", [128, 512], F32)
        gtile = sb("gtileb", [128, 512], F32)
        def ev_lg(m, nci, nw, ps_ap, e):
            e.tensor_copy(out=lg_fm[0:8, nci * 512:nci * 512 + nw], in_=ps_ap)
        gemm(mk, RwT, 1024, 8, lambda k: xn_t[k], [("vector", ev_lg)], mgroup=1)
        ps_t = les.enter_context(nc.psum_tensor(mk.name("pstb"), [128, 8], F32))
        ps_rk = les.enter_context(nc.psum_tensor(mk.name("psrk"), [128, 8], F32))
        ps_pr = les.enter_context(nc.psum_tensor(mk.name("pspr"), [128, 8], F32))
        ps_ct = les.enter_context(nc.psum_tensor(mk.name("psct"), [1, 8], F32))
        ps_cm = les.enter_context(nc.psum_tensor(mk.name("pscmb"), [8, 512], F32))
        ps_rep2 = les.enter_context(nc.psum_tensor(mk.name("prep2b"), [128, 512], F32))
        tri = mk.const("tril_strict", (np.arange(128)[:, None] < np.arange(128)[None, :])
                       .astype(np.float32).astype(bfdt))
        rsel_s = sb("rsels", [8, 8 * 128], BF16)
        onescol_b = sb("onescb2", [128, 1], BF16)
        with nc.Block() as b:
            @b.sync
            def _(e):
                e.dma_start(out=trib[:, :], in_=tri[:, :]).then_inc(s.h, 16)
                e.dma_start(out=cnt_s[:, :], in_=counts_all[:, :]).then_inc(s.h, 16)
                e.dma_start(out=mrow[:, :], in_=maskrow[:, :]).then_inc(s.h, 16)
                for e_ in range(8):
                    e.dma_start(out=rsel_s[:, e_ * 128:(e_ + 1) * 128],
                                in_=mk.rowsel_d(e_)[:, :]).then_inc(s.h, 16)
                    s.inc(16)
                e.wait_ge(s.h, s.inc(48))
            @b.vector
            def _(e):
                e.memset(onesf[:, :], 1.0)
                e.memset(onescol_b[:, :], 1.0)
        with nc.Block() as b:
            @b.vector
            def _(e):
                e.tensor_scalar(out=cnt_s[:, :], in0=cnt_s[:, :], scalar1=mrow[0:8, 0:1],
                                scalar2=None, op0=ALU.mult)
                e.drain()
        with nc.Block() as b:
            @b.tensor
            def _(e):
                e.matmul(ps_ct[0:1, 0:8], mk.onescol_s[0:8, 0:1], cnt_s[:, :], start=True, stop=True)
        with nc.Block() as b:
            @b.vector
            def _(e):
                e.tensor_copy(out=prefix[0:1, :], in_=ps_ct[0:1, 0:8])
        for tt in range(16):
            with nc.Block() as b:
                @b.tensor
                def _(e):
                    e.transpose(ps_t[:, 0:8], lg_fm[0:8, tt * 128:(tt + 1) * 128],
                                mk.identf_s[0:8, 0:8])
            with nc.Block() as b:
                @b.vector
                def _(e):
                    e.reduce_max(out=cv[:, 0:1], in_=ps_t[:, 0:8], axis=AX.X, negate=True)
                    e.drain()
            with nc.Block() as b:
                @b.scalar
                def _(e):
                    e.activation(out=csum[:, :], in_=ps_t[:, 0:8], func=AF.Exp, bias=cv[:, 0:1])
                    e.drain()
            with nc.Block() as b:
                @b.vector
                def _(e):
                    e.reduce_sum(out=cv[:, 1:2], in_=csum[:, :], axis=AX.X)
                    e.drain()
                    e.reciprocal(out=cv[:, 1:2], in_=cv[:, 1:2])
                    e.drain()
                    e.tensor_scalar(out=csum[:, :], in0=csum[:, :], scalar1=cv[:, 1:2],
                                    scalar2=None, op0=ALU.mult)
                    e.drain()
                    e.reduce_max(out=cv[:, 2:3], in_=csum[:, :], axis=AX.X)
                    e.drain()
                    e.tensor_scalar(out=oh[:, :], in0=csum[:, :], scalar1=cv[:, 2:3],
                                    scalar2=None, op0=ALU.is_ge)
                    e.drain()
                    e.tensor_tensor_scan(out=csum[:, :], data0=onesf[:, :], data1=oh[:, :],
                                         initial=0.0, op0=ALU.mult, op1=ALU.add)
                    e.drain()
                    e.tensor_sub(out=csum[:, :], in0=csum[:, :], in1=oh[:, :])
                    e.drain()
                    e.tensor_scalar(out=csum[:, :], in0=csum[:, :], scalar1=0.5,
                                    scalar2=None, op0=ALU.is_lt)
                    e.drain()
                    e.tensor_mul(out=oh[:, :], in0=oh[:, :], in1=csum[:, :])
                    e.drain()
                    e.tensor_copy(out=ohb[:, :], in_=oh[:, :])
                    e.drain()
            with nc.Block() as b:
                @b.tensor
                def _(e):
                    e.matmul(ps_rk[:, 0:8], trib[:, :], ohb[:, :], start=True, stop=True)
                    e.matmul(ps_ct[0:1, 0:8], onescol_b[:, 0:1], ohb[:, :], start=True, stop=True)
                    e.matmul(ps_pr[:, 0:8], mk.ones1f_s[0:1, :], prefix[0:1, :], start=True, stop=True)
            with nc.Block() as b:
                @b.vector
                def _(e):
                    e.tensor_copy(out=csum[:, :], in_=ps_pr[:, 0:8])
                    e.drain()
                    e.tensor_add(out=csum[:, :], in0=csum[:, :], in1=ps_rk[:, 0:8])
                    e.drain()
                    e.tensor_scalar(out=csum[:, :], in0=csum[:, :], scalar1=float(CAP) - 0.5,
                                    scalar2=None, op0=ALU.is_lt)
                    e.drain()
                    e.tensor_mul(out=csum[:, :], in0=oh[:, :], in1=csum[:, :])
                    e.drain()
                    e.reduce_sum(out=cv[:, 3:4], in_=csum[:, :], axis=AX.X)
                    e.drain()
                    e.tensor_mul(out=cv[:, 3:4], in0=cv[:, 3:4], in1=cv[:, 2:3])
                    e.drain()
                    e.tensor_scalar(out=oh[:, :], in0=oh[:, :], scalar1=cv[:, 3:4],
                                    scalar2=None, op0=ALU.mult)
                    e.drain()
                    e.tensor_add(out=prefix[0:1, :], in0=prefix[0:1, :], in1=ps_ct[0:1, 0:8])
                    e.drain()
            with nc.Block() as b:
                @b.tensor
                def _(e):
                    e.transpose(ps_cm[0:8, 0:128], oh[:, :], mk.identf_s[:, :])
            with nc.Block() as b:
                @b.vector
                def _(e):
                    e.tensor_copy(out=comb_fm[0:8, tt * 128:(tt + 1) * 128], in_=ps_cm[0:8, 0:128])
        for nci in range(4):
            sl = slice(nci * 512, nci * 512 + 512)
            def xt_c(k, sl=sl):
                return xn_t[k][:, sl]
            with nc.Block() as b:
                @b.vector
                def _(e):
                    for m in range(8):
                        e.memset(acc[m][:, :], 0.0)
            for e_ in range(8):
                load_param_cols(mk, b1[e_], b1c, kt=32)
                load_param_cols(mk, b2[e_], b2c, kt=8)
                with nc.Block() as b:
                    @b.tensor
                    def _(e):
                        e.matmul(ps_rep2[:, :], rsel_s[:, e_ * 128:(e_ + 1) * 128],
                                 comb_fm[0:8, sl], start=True, stop=True)
                with nc.Block() as b:
                    @b.vector
                    def _(e):
                        e.tensor_copy(out=comb_rep[:, :], in_=ps_rep2[:, :])
                def ev_he(m, nci2, nw, ps_ap, e):
                    e.activation(out=he[m][:, 0:nw], in_=ps_ap, func=AF.Gelu,
                                 bias=b1c[:, m:m + 1])
                gemm(mk, W1[e_], 1024, 4096, xt_c, [("scalar", ev_he)], n_total=512, mgroup=1)
                with nc.Block() as b:
                    @b.vector
                    def _(e):
                        for m in range(32):
                            e.tensor_mul(out=he[m][:, :], in0=he[m][:, :], in1=comb_rep[:, :])
                def ev_acc(m, nci2, nw, ps_ap, e):
                    e.tensor_scalar(out=tmpf[:, 0:nw], in0=comb_rep[:, 0:nw], scalar1=b2c[:, m:m + 1],
                                    scalar2=None, op0=ALU.mult)
                    e.drain()
                    e.tensor_add(out=tmpf[:, 0:nw], in0=tmpf[:, 0:nw], in1=ps_ap)
                    e.drain()
                    e.tensor_add(out=acc[m][:, 0:nw], in0=acc[m][:, 0:nw], in1=tmpf[:, 0:nw])
                    e.drain()
                gemm(mk, W2[e_], 4096, 1024, lambda k: he[k], [("vector", ev_acc)],
                     n_total=512, mgroup=1)
            with nc.Block() as b:
                @b.scalar
                def _(e):
                    for m in range(8):
                        e.activation(out=acc[m][:, :], in_=acc[m][:, :], func=AF.Silu)
            with nc.Block() as b:
                @b.vector
                def _(e):
                    for m in range(8):
                        e.tensor_copy(out=gtile[:, :], in_=s_t[m][:, sl])
                        e.drain()
                        e.tensor_mul(out=tmpf[:, :], in0=acc[m][:, :], in1=gtile[:, :])
                        e.drain()
                        e.tensor_add(out=x_t[m][:, sl], in0=x_t[m][:, sl], in1=tmpf[:, :])
                        e.drain()
    with nc.Block() as b:
        @b.sync
        def _(e):
            for k in range(8):
                e.dma_start(out=x_out[k * 128:(k + 1) * 128, :], in_=x_t[k][:, :]).then_inc(s.h, 16)
                s.inc(16)
            e.wait_ge(s.h, s.v)
    return mk.finish()





NH = 32
HP = 64
D_INNER = 2048


def cc_loop(mk, nc, cac, pre_t, xb_t, cw_s, cb_s):
    for j in range(18):
        with nc.Block() as b:
            @b.vector
            def _(e):
                e.tensor_scalar(out=cac[:, :], in0=pre_t[j][:, 1:513],
                                scalar1=cw_s[:, 4 * j:4 * j + 1], scalar2=None, op0=ALU.mult)
                e.drain()
                for i in range(1, 4):
                    e.scalar_tensor_tensor(out=cac[:, :], in0=pre_t[j][:, 1 + i:1 + i + 512],
                                           scalar=cw_s[:, 4 * j + i:4 * j + i + 1],
                                           in1=cac[:, :], op0=ALU.mult, op1=ALU.add)
                    e.drain()
        with nc.Block() as b:
            @b.scalar
            def _(e):
                e.activation(out=xb_t[j][:, :], in_=cac[:, :], func=AF.Silu,
                             bias=cb_s[:, j:j + 1])
        with nc.Block() as b:
            @b.vector
            def _(e):
                e.tensor_copy(out=pre_t[j][:, 1:4], in_=pre_t[j][:, 513:516])


def emit_l2_attn(mk, x_t, xn_t, mode, s_t=None):
    nc = mk.nc
    s = mk.sem("dma")
    W_in = mk.din("W_in", (4384, 1024))
    conv_w = mk.din("conv_w", (2304, 4))
    conv_b = mk.din("conv_b", (2304,))
    dt_bias = mk.din("dt_bias", (32,))
    A_log = mk.din("A_log", (32,))
    D_par = mk.din("D_par", (32,))
    xh = mk.din("xh", (D, 3))
    if mode == "B":
        norm_w = mk.din("norm_w", (2048,))
        W_out = mk.din("W_out", (1024, 2048))
        E_in = [mk.din(f"E{j}", (128, 2048)) for j in range(3)]
        P_in = [mk.din(f"P{j}", (128, 32)) for j in range(3)]
    WinT = prep_w(mk, W_in, "oi", 1024, 4384, tag="l2win")
    if mode == "B":
        WoT = prep_w(mk, W_out, "oi", 2048, 1024, tag="l2wo")
    if mode == "A":
        E_o = mk.dout("E_out", (128, 2048))
        P_o = mk.dout("P_out", (128, 32))

    xh_t = [mk.sbuf("xh", [128, 4], F32) for _ in range(8)]
    xnh_t = [mk.sbuf("xnh", [128, 4], BF16) for _ in range(8)]
    with nc.Block() as b:
        @b.sync
        def _(e):
            for k in range(8):
                e.dma_start(out=xh_t[k][:, 0:3], in_=xh[k * 128:(k + 1) * 128, :]).then_inc(s.h, 16)
                s.inc(16)
            e.wait_ge(s.h, s.v)
    layernorm(mk, xh_t, xnh_t, n_total=3)

    cw_s = mk.sbuf("cw", [128, 4 * 18], F32)
    cb_s = mk.sbuf("cb", [128, 18], F32)
    dtb_s = mk.sbuf("dtb", [32, 1], F32)
    alog_s = mk.sbuf("alog", [32, 1], F32)
    nega_s = mk.sbuf("nega", [32, 1], F32)
    dall_s = mk.sbuf("dall", [128, 32], F32)
    dcol_s = mk.sbuf("dcol", [32, 1], F32)
    ones32 = mk.sbuf("ones32", [32, 128], F32)
    causm = mk.sbuf("causm", [128, 128], F32)
    nwc = mk.sbuf("nwc", [128, 16], F32)
    causc = mk.const("causm_f32", np.where(np.arange(128)[:, None] <= np.arange(128)[None, :], 0.0, -1e30).astype(np.float32))
    with nc.Block() as b:
        @b.sync
        def _(e):
            for j in range(18):
                e.dma_start(out=cw_s[:, 4 * j:4 * j + 4], in_=conv_w[j * 128:(j + 1) * 128, :]).then_inc(s.h, 16)
                e.dma_start(out=cb_s[:, j:j + 1],
                            in_=conv_b[j * 128:(j + 1) * 128].rearrange("(p o) -> p o", o=1)).then_inc(s.h, 16)
                s.inc(32)
            e.dma_start(out=dtb_s[:, :], in_=dt_bias[:].rearrange("(p o) -> p o", o=1)).then_inc(s.h, 16)
            e.dma_start(out=alog_s[:, :], in_=A_log[:].rearrange("(p o) -> p o", o=1)).then_inc(s.h, 16)
            e.dma_start(out=dcol_s[:, :], in_=D_par[:].rearrange("(p o) -> p o", o=1)).then_inc(s.h, 16)
            e.dma_start(out=causm[:, :], in_=causc[:, :]).then_inc(s.h, 16)
            s.inc(4 * 16)
            e.wait_ge(s.h, s.v)
    if mode == "B":
        load_param_cols(mk, norm_w, nwc, kt=16)
    with nc.Block() as b:
        @b.scalar
        def _(e):
            e.activation(out=nega_s[:, :], in_=alog_s[:, :], func=AF.Exp)
            e.drain()
        @b.vector
        def _(e):
            e.memset(ones32[:, :], 1.0)
    with nc.Block() as b:
        @b.vector
        def _(e):
            e.tensor_scalar_mul(out=nega_s[:, :], in0=nega_s[:, :], scalar1=-1.0)
            e.drain()
    with ExitStack() as les0:
        ps_d = les0.enter_context(nc.psum_tensor(mk.name("psd"), [128, 32], F32))
        ps_d1 = les0.enter_context(nc.psum_tensor(mk.name("psd1"), [1, 32], F32))
        dr1 = les0.enter_context(nc.sbuf_tensor(mk.name("dr1"), [1, 32], F32))
        with nc.Block() as b:
            @b.tensor
            def _(e):
                e.transpose(ps_d1[0:1, 0:32], dcol_s[0:32, 0:1], mk.identf_s[0:32, 0:32])
        with nc.Block() as b:
            @b.vector
            def _(e):
                e.tensor_copy(out=dr1[0:1, :], in_=ps_d1[0:1, 0:32])
        with nc.Block() as b:
            @b.tensor
            def _(e):
                e.matmul(ps_d[:, :], mk.ones1f_s[0:1, :], dr1[0:1, :], start=True, stop=True)
        with nc.Block() as b:
            @b.vector
            def _(e):
                e.tensor_copy(out=dall_s[:, :], in_=ps_d[:, :])

    with ExitStack() as main:
        msb = lambda nm, shp, dt: main.enter_context(nc.sbuf_tensor(mk.name(nm), shp, dt))
        hst = msb("hst", [128, NH * HP], F32)
        pacc = msb("pacc", [32, 1], F32)
        with nc.Block() as b:
            @b.vector
            def _(e):
                e.memset(hst[:, :], 0.0)
                e.memset(pacc[:, :], 0.0)
                e.drain()
        if mode == "B":
            with ExitStack() as lese:
                Ebuf = lese.enter_context(nc.sbuf_tensor(mk.name("ebuf"), [128, 2048], F32))
                Pbuf = lese.enter_context(nc.sbuf_tensor(mk.name("pbuf"), [128, 32], F32))
                for j in (2, 1, 0):
                    with nc.Block() as b:
                        @b.sync
                        def _(e):
                            e.dma_start(out=Ebuf[:, :], in_=E_in[j][:, :]).then_inc(s.h, 16)
                            e.dma_start(out=Pbuf[:, :], in_=P_in[j][:, :]).then_inc(s.h, 16)
                            e.wait_ge(s.h, s.inc(32))
                    with nc.Block() as b:
                        @b.vector
                        def _(e):
                            for h in range(NH):
                                hs = slice(h * HP, (h + 1) * HP)
                                e.tensor_scalar(out=hst[:, hs], in0=hst[:, hs],
                                                scalar1=Pbuf[:, h:h + 1], scalar2=None, op0=ALU.mult)
                            e.drain()
                            for h in range(NH):
                                hs = slice(h * HP, (h + 1) * HP)
                                e.tensor_add(out=hst[:, hs], in0=hst[:, hs], in1=Ebuf[:, hs])
                            e.drain()
        z_t = [msb("z", [128, 512], BF16) for _ in range(16)]
        pre_t = [msb("pre", [128, 516], BF16) for _ in range(18)]
        xb_t = [msb("xb", [128, 512], BF16) for _ in range(18)]
        y_t = [pre_t[j][:, 4:516] for j in range(16)]
        xtm = msb("xtm", [128, 16 * 128], BF16)
        btm = msb("btm", [128, 128], BF16)
        delta_s = msb("delta", [32, 512], F32)
        Lfm_s = msb("Lfm", [32, 512], F32)
        ps_ld = main.enter_context(nc.psum_tensor(mk.name("psld"), [32, 512], F32))
        Ltm = msb("Ltm", [128, 32], F32)
        dtm = msb("dtm", [128, 32], F32)
        lendr = msb("lendr", [128, 32], F32)
        wcols = msb("wcols", [128, 32], F32)
        pendr = msb("pendr", [128, 32], F32)
        bcs = msb("bcs", [128, 128], BF16)
        mxx = msb("mxx", [128, 128], F32)
        mexp = msb("mexp", [128, 128], BF16)
        reptm = mxx
        ptexp = msb("ptexp", [128, 128], BF16)
        mbc = msb("mbc", [128, 128], BF16)
        crep = msb("crep", [128, 128], BF16)
        btmw = msb("btmw", [128, 128], BF16)
        hstb = msb("hstb", [128, 64], BF16)
        sdtmp = mxx
        hstt = msb("hstt", [128, 64], F32)
        l1r = msb("l1r", [1, 32], F32)

        for mc in range(4):
            sl = slice(mc * 512, mc * 512 + 512)
            def xt_c(k, sl=sl):
                return xn_t[k][:, sl]
            def ev_zx(m, nci, nw, ps_ap, e):
                if m < 16:
                    e.copy(out=z_t[m][:, 0:nw], in_=ps_ap)
                elif m < 34:
                    e.copy(out=pre_t[m - 16][:, 4:4 + nw], in_=ps_ap)
            gemm(mk, WinT, 1024, 4384, xt_c, [("scalar", ev_zx)], n_total=512, mgroup=1,
                 m_stop=34)
            def ev_dt(m, nci, nw, ps_ap, e):
                e.tensor_copy(out=delta_s[0:32, 0:nw], in_=ps_ap)
            gemm(mk, WinT, 1024, 4384, xt_c, [("vector", ev_dt)], n_total=512, mgroup=1,
                 m_start=34, m_stop=35)
            if mc == 0:
                def xt_h(k):
                    return xnh_t[k][:, 0:3]
                def ev_h(m, nci, nw, ps_ap, e):
                    e.copy(out=pre_t[m - 16][:, 1:4], in_=ps_ap)
                gemm(mk, WinT, 1024, 4384, xt_h, [("scalar", ev_h)], n_total=3, mgroup=1,
                     m_start=16, m_stop=34)
            with nc.Block() as b:
                @b.scalar
                def _(e):
                    e.activation(out=ps_ld[0:32, 0:512], in_=delta_s[0:32, 0:512],
                                 func=AF.Identity, bias=dtb_s[0:32, 0:1])
                    e.drain()
                    e.activation(out=delta_s[0:32, 0:512], in_=ps_ld[0:32, 0:512], func=AF.Abs)
                    e.drain()
                    e.activation(out=delta_s[0:32, 0:512], in_=delta_s[0:32, 0:512], func=AF.Exp, scale=-1.0)
                    e.drain()
                    e.activation(out=delta_s[0:32, 0:512], in_=delta_s[0:32, 0:512], func=AF.Ln, bias=1.0)
                    e.drain()
            with ExitStack() as lsp:
                ps_sp = lsp.enter_context(nc.psum_tensor(mk.name("pssp"), [32, 512], F32))
                with nc.Block() as b:
                    @b.vector
                    def _(e):
                        e.tensor_scalar_max(out=ps_sp[0:32, :], in0=ps_ld[0:32, 0:512], scalar1=0.0)
                        e.drain()
                        e.tensor_add(out=delta_s[0:32, 0:512], in0=delta_s[0:32, 0:512], in1=ps_sp[0:32, :])
                        e.drain()
                        e.tensor_scalar(out=ps_ld[0:32, 0:512], in0=delta_s[0:32, 0:512],
                                        scalar1=nega_s[0:32, 0:1], scalar2=None, op0=ALU.mult)
                        e.drain()
            with nc.sbuf_tensor(mk.name("cacc"), [128, 512], F32) as cac:
                cc_loop(mk, nc, cac, pre_t, xb_t, cw_s, cb_s)
            if False:
              for j in range(18):
                with nc.Block() as b:
                    @b.vector
                    def _(e):
                        e.tensor_scalar(out=cac[:, :], in0=pre_t[j][:, 0:512],
                                        scalar1=cw_s[:, 4 * j:4 * j + 1], scalar2=None, op0=ALU.mult)
                        e.drain()
                        for i in range(1, 4):
                            e.scalar_tensor_tensor(out=cac[:, :], in0=pre_t[j][:, i:i + 512],
                                                   scalar=cw_s[:, 4 * j + i:4 * j + i + 1],
                                                   in1=cac[:, :], op0=ALU.mult, op1=ALU.add)
                            e.drain()
                with nc.Block() as b:
                    @b.scalar
                    def _(e):
                        e.activation(out=xb_t[j][:, :], in_=cac[:, :], func=AF.Silu,
                                     bias=cb_s[:, j:j + 1])
                with nc.Block() as b:
                    @b.vector
                    def _(e):
                        e.tensor_copy(out=pre_t[j][:, 1:4], in_=pre_t[j][:, 513:516])
            with ExitStack() as les2:
                ps_tr = les2.enter_context(nc.psum_tensor(mk.name("pstr"), [128, 1024], BF16))
                ps_bc = les2.enter_context(nc.psum_tensor(mk.name("psbc"), [128, 128], F32))
                ps_y = les2.enter_context(nc.psum_tensor(mk.name("psy"), [64, 128], F32))
                ps_st = les2.enter_context(nc.psum_tensor(mk.name("psst"), [128, 64], F32))
                ps_rep = les2.enter_context(nc.psum_tensor(mk.name("psrep"), [128, 128], F32))
                ps_sm = les2.enter_context(nc.psum_tensor(mk.name("pssm"), [128, 32], F32))
                for sc in range(4):
                    t0 = sc * 128
                    tsl = slice(t0, t0 + 128)
                    with nc.Block() as b:
                        @b.vector
                        def _(e):
                            e.tensor_tensor_scan(out=Lfm_s[0:32, tsl], data0=ones32[0:32, 0:128],
                                                 data1=ps_ld[0:32, tsl], initial=0.0,
                                                 op0=ALU.mult, op1=ALU.add)
                            e.drain()
                    with nc.Block() as b:
                        @b.tensor
                        def _(e):
                            e.transpose(ps_sm[:, 0:32], Lfm_s[0:32, tsl], mk.identf_s[0:32, 0:32])
                    with nc.Block() as b:
                        @b.vector
                        def _(e):
                            e.tensor_copy(out=Ltm[:, :], in_=ps_sm[:, 0:32])
                    with nc.Block() as b:
                        @b.tensor
                        def _(e):
                            e.transpose(ps_sm[:, 0:32], delta_s[0:32, tsl], mk.identf_s[0:32, 0:32])
                    with nc.Block() as b:
                        @b.vector
                        def _(e):
                            e.tensor_copy(out=dtm[:, :], in_=ps_sm[:, 0:32])
                            e.drain()
                    with nc.Block() as b:
                        @b.tensor
                        def _(e):
                            e.transpose(ps_sm[0:1, 0:32], Lfm_s[0:32, t0 + 127:t0 + 128], mk.identf_s[0:32, 0:32])
                    with nc.Block() as b:
                        @b.vector
                        def _(e):
                            e.tensor_copy(out=l1r[0:1, :], in_=ps_sm[0:1, 0:32])
                    with nc.Block() as b:
                        @b.tensor
                        def _(e):
                            e.matmul(ps_sm[:, 0:32], mk.ones1f_s[0:1, :], l1r[0:1, :],
                                     start=True, stop=True)
                    with nc.Block() as b:
                        @b.vector
                        def _(e):
                            e.tensor_copy(out=lendr[:, :], in_=ps_sm[:, 0:32])
                            e.drain()
                            e.tensor_sub(out=wcols[:, :], in0=lendr[:, :], in1=Ltm[:, :])
                            e.drain()
                    with nc.Block() as b:
                        @b.scalar
                        def _(e):
                            e.activation(out=wcols[:, :], in_=wcols[:, :], func=AF.Exp)
                            e.drain()
                            e.activation(out=pendr[:, :], in_=lendr[:, :], func=AF.Exp)
                            e.drain()
                    with nc.Block() as b:
                        @b.tensor
                        def _(e):
                            e.transpose(ps_tr[:, 0:128], xb_t[16][:, tsl], mk.ident_s[:, :])
                    with nc.Block() as b:
                        @b.vector
                        def _(e):
                            e.tensor_copy(out=btm[:, :], in_=ps_tr[:, 0:128])
                    for j0 in range(0, 16, 8):
                        with nc.Block() as b:
                            @b.tensor
                            def _(e):
                                for j in range(j0, j0 + 8):
                                    e.transpose(ps_tr[:, (j - j0) * 128:(j - j0 + 1) * 128],
                                                xb_t[j][:, tsl], mk.ident_s[:, :])
                        with nc.Block() as b:
                            @b.vector
                            def _(e):
                                for j in range(j0, j0 + 8):
                                    e.tensor_copy(out=xtm[:, j * 128:(j + 1) * 128],
                                                  in_=ps_tr[:, (j - j0) * 128:(j - j0 + 1) * 128])
                    with nc.Block() as b:
                        @b.vector
                        def _(e):
                            for j in range(16):
                                e.tensor_scalar(out=xtm[:, j * 128:j * 128 + 64],
                                                in0=xtm[:, j * 128:j * 128 + 64],
                                                scalar1=dtm[:, 2 * j:2 * j + 1], scalar2=None, op0=ALU.mult)
                                e.tensor_scalar(out=xtm[:, j * 128 + 64:j * 128 + 128],
                                                in0=xtm[:, j * 128 + 64:j * 128 + 128],
                                                scalar1=dtm[:, 2 * j + 1:2 * j + 2], scalar2=None, op0=ALU.mult)
                            e.drain()
                    with nc.Block() as b:
                        @b.tensor
                        def _(e):
                            e.matmul(ps_bc[:, :], xb_t[16][:, tsl], xb_t[17][:, tsl],
                                     start=True, stop=True)
                    with nc.Block() as b:
                        @b.vector
                        def _(e):
                            e.tensor_copy(out=bcs[:, :], in_=ps_bc[:, :])
                    for h in range(NH):
                        jt = h // 2
                        hof = (h % 2) * 64
                        with nc.Block() as b:
                            @b.vector
                            def _(e):
                                e.memset(reptm[:, :], 1.0)
                                e.drain()
                                e.tensor_scalar(out=reptm[:, :], in0=reptm[:, :],
                                                scalar1=Ltm[:, h:h + 1], scalar2=None, op0=ALU.mult)
                                e.drain()
                        with nc.Block() as b:
                            @b.tensor
                            def _(e):
                                e.transpose(ps_rep[:, :], reptm[:, :], mk.identf_s[:, :])
                        with nc.Block() as b:
                            @b.vector
                            def _(e):
                                e.tensor_scalar(out=mxx[:, :], in0=ps_rep[:, :],
                                                scalar1=Ltm[:, h:h + 1], scalar2=None, op0=ALU.subtract)
                                e.drain()
                                e.tensor_add(out=mxx[:, :], in0=mxx[:, :], in1=causm[:, :])
                                e.drain()
                                e.tensor_copy(out=hstb[:, :], in_=hst[:, h * HP:(h + 1) * HP])
                                e.drain()
                        with nc.Block() as b:
                            @b.scalar
                            def _(e):
                                e.activation(out=mexp[:, :], in_=mxx[:, :], func=AF.Exp)
                                e.drain()
                                e.activation(out=ptexp[:, :], in_=ps_rep[:, :], func=AF.Exp)
                                e.drain()
                        with nc.Block() as b:
                            @b.vector
                            def _(e):
                                e.tensor_mul(out=mbc[:, :], in0=mexp[:, :], in1=bcs[:, :])
                                e.drain()
                                e.tensor_mul(out=crep[:, :], in0=xb_t[17][:, tsl], in1=ptexp[:, :])
                                e.drain()
                                e.tensor_scalar(out=btmw[:, :], in0=btm[:, :],
                                                scalar1=wcols[:, h:h + 1], scalar2=None, op0=ALU.mult)
                                e.drain()
                        with nc.Block() as b:
                            @b.tensor
                            def _(e):
                                e.matmul(ps_y[:, :], xtm[:, jt * 128 + hof: jt * 128 + hof + 64],
                                         mbc[:, :], start=True, stop=False)
                                e.matmul(ps_y[:, :], hstb[:, :], crep[:, :],
                                         start=False, stop=True)
                                e.matmul(ps_st[:, :], btmw[:, :],
                                         xtm[:, jt * 128 + hof: jt * 128 + hof + 64],
                                         start=True, stop=True)
                        with nc.Block() as b:
                            @b.vector
                            def _(e):
                                xsrc = xb_t[jt][0:64, tsl] if hof == 0 else xb_t[jt][64:128, tsl]
                                e.tensor_scalar(out=sdtmp[0:64, 0:128], in0=xsrc,
                                                scalar1=dall_s[0:64, h:h + 1],
                                                scalar2=None, op0=ALU.mult)
                                e.drain()
                                e.tensor_add(out=sdtmp[0:64, 0:128], in0=sdtmp[0:64, 0:128], in1=ps_y[:, :])
                                e.drain()
                                e.tensor_copy(out=(y_t[jt][0:64, tsl] if hof == 0 else y_t[jt][64:128, tsl]),
                                              in_=sdtmp[0:64, 0:128])
                                e.drain()
                                e.tensor_scalar(out=hstt[:, :], in0=hst[:, h * HP:(h + 1) * HP],
                                                scalar1=pendr[:, h:h + 1], scalar2=None, op0=ALU.mult)
                                e.drain()
                                e.tensor_add(out=hst[:, h * HP:(h + 1) * HP], in0=hstt[:, :], in1=ps_st[:, :])
                                e.drain()
                    with nc.Block() as b:
                        @b.vector
                        def _(e):
                            e.tensor_add(out=pacc[0:32, 0:1], in0=pacc[0:32, 0:1],
                                         in1=Lfm_s[0:32, t0 + 127:t0 + 128])
                            e.drain()
            if mode == "B":
                with ExitStack() as les3:
                    sq2 = les3.enter_context(nc.sbuf_tensor(mk.name("sq2"), [128, 512], F32))
                    rr = delta_s[0:1, 0:512]
                    rrep = les3.enter_context(nc.sbuf_tensor(mk.name("rrep"), [128, 512], F32))
                    ps_s2 = les3.enter_context(nc.psum_tensor(mk.name("pss2"), [1, 512], F32))
                    ps_rr = les3.enter_context(nc.psum_tensor(mk.name("psrr"), [128, 512], F32))
                    for j in range(16):
                        with nc.Block() as b:
                            @b.scalar
                            def _(e):
                                e.activation(out=sq2[:, :], in_=z_t[j][:, :], func=AF.Silu)
                        with nc.Block() as b:
                            @b.vector
                            def _(e):
                                e.tensor_mul(out=y_t[j][:, :], in0=y_t[j][:, :], in1=sq2[:, :])
                    for j in range(16):
                        with nc.Block() as b:
                            @b.scalar
                            def _(e):
                                e.activation(out=sq2[:, :], in_=y_t[j][:, :], func=AF.Square)
                        with nc.Block() as b:
                            @b.tensor
                            def _(e):
                                e.matmul(ps_s2[0:1, :], mk.onescol_s[:, :], sq2[:, :],
                                         start=(j == 0), stop=(j == 15))
                    with nc.Block() as b:
                        @b.vector
                        def _(e):
                            e.tensor_scalar(out=rr, in0=ps_s2[0:1, :], scalar1=1.0 / 2048.0,
                                            scalar2=1e-5, op0=ALU.mult, op1=ALU.add)
                            e.drain()
                    with nc.Block() as b:
                        @b.scalar
                        def _(e):
                            e.activation(out=rr, in_=rr, func=AF.Ln)
                            e.drain()
                            e.activation(out=rr, in_=rr, func=AF.Exp, scale=-0.5)
                            e.drain()
                    with nc.Block() as b:
                        @b.tensor
                        def _(e):
                            e.matmul(ps_rr[:, :], mk.ones1f_s[0:1, :], rr, start=True, stop=True)
                    with nc.Block() as b:
                        @b.vector
                        def _(e):
                            e.tensor_copy(out=rrep[:, :], in_=ps_rr[:, :])
                            e.drain()
                    for j in range(16):
                        with nc.Block() as b:
                            @b.vector
                            def _(e):
                                e.tensor_mul(out=sq2[:, :], in0=y_t[j][:, :], in1=rrep[:, :])
                                e.drain()
                                e.tensor_scalar(out=y_t[j][:, :], in0=sq2[:, :], scalar1=nwc[:, j:j + 1],
                                                scalar2=None, op0=ALU.mult)
                                e.drain()
                def ev_s(m, nci, nw, ps_ap, e, sl=sl):
                    e.copy(out=s_t[m][:, sl.start:sl.start + nw], in_=ps_ap)
                gemm(mk, WoT, 2048, 1024, lambda k: y_t[k], [("scalar", ev_s)],
                     n_total=512, mgroup=1)
        if mode == "A":
            with ExitStack() as les4:
                ps_p = les4.enter_context(nc.psum_tensor(mk.name("psp"), [128, 32], F32))
                ps_p1 = les4.enter_context(nc.psum_tensor(mk.name("psp1"), [1, 32], F32))
                with nc.Block() as b:
                    @b.tensor
                    def _(e):
                        e.transpose(ps_p1[0:1, 0:32], pacc[0:32, 0:1], mk.identf_s[0:32, 0:32])
                with nc.Block() as b:
                    @b.vector
                    def _(e):
                        e.tensor_copy(out=l1r[0:1, :], in_=ps_p1[0:1, 0:32])
                with nc.Block() as b:
                    @b.tensor
                    def _(e):
                        e.matmul(ps_p[:, :], mk.ones1f_s[0:1, :], l1r[0:1, :], start=True, stop=True)
                with nc.Block() as b:
                    @b.vector
                    def _(e):
                        e.tensor_copy(out=lendr[:, :], in_=ps_p[:, :])
                        e.drain()
                with nc.Block() as b:
                    @b.scalar
                    def _(e):
                        e.activation(out=lendr[:, :], in_=lendr[:, :], func=AF.Exp)
                        e.drain()
                with nc.Block() as b:
                    @b.sync
                    def _(e):
                        e.dma_start(out=E_o[:, :], in_=hst[:, :]).then_inc(s.h, 16)
                        e.dma_start(out=P_o[:, :], in_=lendr[:, :]).then_inc(s.h, 16)
                        e.wait_ge(s.h, s.inc(32))


def build_l2a():
    mk = MK()
    x_in = mk.din("x", (D, T))
    setup_common(mk)
    x_t = [mk.sbuf("x", [128, T], F32) for _ in range(8)]
    xn_t = [mk.sbuf("xn", [128, T], BF16) for _ in range(8)]
    load_x(mk, x_in, x_t)
    layernorm(mk, x_t, xn_t)
    emit_l2_attn(mk, x_t, xn_t, "A")
    return mk.finish()


def emit_l2_ffn(mk, x_t, xn_t, s_t):
    nc = mk.nc
    s = mk.sem("dma")
    w1w = mk.din("w1_w", (4096, 1024))
    w1s = mk.din("w1_s", (4096,))
    w2w = mk.din("w2_w", (4096, 1024))
    w2s = mk.din("w2_s", (4096,))
    h1w = mk.din("h1_w", (4096, 4096))
    h1s = mk.din("h1_s", (4096,))
    h2w = mk.din("h2_w", (4096, 4096))
    h2s = mk.din("h2_s", (4096,))
    w3w = mk.din("w3_w", (1024, 4096))
    w3s = mk.din("w3_s", (1024,))
    W1 = prep_w(mk, w1w, "oi", 1024, 4096, tag="f2w1")
    W2 = prep_w(mk, w2w, "oi", 1024, 4096, tag="f2w2")
    H1 = prep_w(mk, h1w, "oi", 4096, 4096, tag="f2h1")
    H2 = prep_w(mk, h2w, "oi", 4096, 4096, tag="f2h2")
    W3 = prep_w(mk, w3w, "oi", 4096, 1024, tag="f2w3")
    with ExitStack() as les:
        sb = lambda nm, shp, dt: les.enter_context(nc.sbuf_tensor(mk.name(nm), shp, dt))
        s1c = sb("s1c", [128, 32], F32)
        s2c = sb("s2c", [128, 32], F32)
        hs1c = sb("hs1c", [128, 32], F32)
        hs2c = sb("hs2c", [128, 32], F32)
        s3c = sb("s3c", [128, 8], F32)
        hA = [sb("hA", [128, 512], BF16) for _ in range(32)]
        hB = [sb("hB", [128, 512], BF16) for _ in range(32)]
        gtile = sb("g2", [128, 512], BF16)
        tmpf = sb("tf2", [128, 512], F32)
        load_param_cols(mk, w1s, s1c, kt=32)
        load_param_cols(mk, w2s, s2c, kt=32)
        load_param_cols(mk, h1s, hs1c, kt=32)
        load_param_cols(mk, h2s, hs2c, kt=32)
        load_param_cols(mk, w3s, s3c, kt=8)
        for nci in range(4):
            sl = slice(nci * 512, nci * 512 + 512)
            def xt_c(k, sl=sl):
                return xn_t[k][:, sl]
            def ev_t1(m, nci2, nw, ps_ap, e):
                e.activation(out=hA[m][:, 0:nw], in_=ps_ap, func=AF.Silu,
                             scale=s1c[:, m:m + 1])
            gemm(mk, W1, 1024, 4096, xt_c, [("scalar", ev_t1)], n_total=512, mgroup=1)
            def ev_t2(m, nci2, nw, ps_ap, e):
                e.activation(out=hB[m][:, 0:nw], in_=ps_ap, func=AF.Identity,
                             scale=s2c[:, m:m + 1])
            gemm(mk, W2, 1024, 4096, xt_c, [("scalar", ev_t2)], n_total=512, mgroup=1)
            with nc.Block() as b:
                @b.vector
                def _(e):
                    for m in range(32):
                        e.tensor_mul(out=hA[m][:, :], in0=hA[m][:, :], in1=hB[m][:, :])
            def ev_g1(m, nci2, nw, ps_ap, e):
                e.activation(out=hB[m][:, 0:nw], in_=ps_ap, func=AF.Gelu,
                             scale=hs1c[:, m:m + 1])
            gemm(mk, H1, 4096, 4096, lambda k: hA[k][:, :], [("scalar", ev_g1)],
                 n_total=512, mgroup=1)
            def ev_g2(m, nci2, nw, ps_ap, e):
                e.activation(out=hA[m][:, 0:nw], in_=ps_ap, func=AF.Gelu,
                             scale=hs2c[:, m:m + 1])
            gemm(mk, H2, 4096, 4096, lambda k: hB[k][:, :], [("scalar", ev_g2)],
                 n_total=512, mgroup=1)
            def ev_g(m, nci2, nw, ps_ap, e, sl=sl):
                e.activation(out=gtile[:, 0:nw], in_=ps_ap, func=AF.Silu,
                             scale=s3c[:, m:m + 1])
                e.drain()
            def ev_gv(m, nci2, nw, ps_ap, e, sl=sl):
                e.tensor_copy(out=tmpf[:, 0:nw], in_=s_t[m][:, sl])
                e.drain()
                e.tensor_mul(out=tmpf[:, 0:nw], in0=tmpf[:, 0:nw], in1=gtile[:, 0:nw])
                e.drain()
                e.tensor_add(out=x_t[m][:, sl], in0=x_t[m][:, sl], in1=tmpf[:, 0:nw])
                e.drain()
            gemm(mk, W3, 4096, 1024, lambda k: hA[k][:, :], [("scalar", ev_g), ("vector", ev_gv)],
                 n_total=512, mgroup=1)


def build_l2b():
    mk = MK()
    x_in = mk.din("x", (D, T))
    x_out = mk.dout("x_new", (D, T))
    setup_common(mk)
    x_t = [mk.sbuf("x", [128, T], F32) for _ in range(8)]
    xn_t = [mk.sbuf("xn", [128, T], BF16) for _ in range(8)]
    s_t = [mk.sbuf("s2", [128, T], BF16) for _ in range(8)]
    load_x(mk, x_in, x_t)
    layernorm(mk, x_t, xn_t)
    emit_l2_attn(mk, x_t, xn_t, "B", s_t=s_t)
    emit_l2_ffn(mk, x_t, xn_t, s_t)
    s = mk.sem("dma")
    nc = mk.nc
    with nc.Block() as b:
        @b.sync
        def _(e):
            for k in range(8):
                e.dma_start(out=x_out[k * 128:(k + 1) * 128, :], in_=x_t[k][:, :]).then_inc(s.h, 16)
                s.inc(16)
            e.wait_ge(s.h, s.v)
    return mk.finish()





DST = 16
DTR = 64


def emit_l3_attn(mk, x_t, xn_t, mode, s_t=None):
    nc = mk.nc
    s = mk.sem("dma")
    W_in = mk.din("W_in", (4096, 1024))
    cw1 = mk.din("conv_w1", (2048, 4))
    cb1 = mk.din("conv_b1", (2048,))
    cw2 = mk.din("conv_w2", (2048, 4))
    cb2 = mk.din("conv_b2", (2048,))
    cg = mk.din("conv_gates", (2,))
    W_x = mk.din("W_x", (96, 2048))
    W_dt = mk.din("W_dt", (2048, 64))
    b_dt = mk.din("b_dt", (2048,))
    A_log = mk.din("A_log", (2048, 16))
    D_param = mk.din("D_param", (2048,))
    xh = mk.din("xh", (D, 6))
    if mode == "B":
        W_out = mk.din("W_out", (1024, 2048))
        E_in = [mk.din(f"E{j}", (128, 256)) for j in range(3)]
        P_in = [mk.din(f"P{j}", (128, 256)) for j in range(3)]
    WinT = prep_w(mk, W_in, "oi", 1024, 4096, tag="l3win")
    WxTa = prep_w(mk, W_x, "oi", 2048, 80, tag="l3wxa")
    WxTc = prep_w(mk, W_x, "oi", 2048, 16, row_off=80, tag="l3wxc")
    WdT = prep_w(mk, W_dt, "oi", 64, 2048, tag="l3wd")
    if mode == "B":
        WoT = prep_w(mk, W_out, "oi", 2048, 1024, tag="l3wo")
    if mode == "A":
        E_o = mk.dout("E_out", (128, 256))
        P_o = mk.dout("P_out", (128, 256))

    xh_t = [mk.sbuf("xh3", [128, 8], F32) for _ in range(8)]
    xnh_t = [mk.sbuf("xnh3", [128, 8], BF16) for _ in range(8)]
    with nc.Block() as b:
        @b.sync
        def _(e):
            for k in range(8):
                e.dma_start(out=xh_t[k][:, 0:6], in_=xh[k * 128:(k + 1) * 128, :]).then_inc(s.h, 16)
                s.inc(16)
            e.wait_ge(s.h, s.v)
    layernorm(mk, xh_t, xnh_t, n_total=6)

    cw1_s = mk.sbuf("cw1", [128, 4 * 16], F32)
    cw2_s = mk.sbuf("cw2", [128, 4 * 16], F32)
    cb1_s = mk.sbuf("cb1", [128, 16], F32)
    cb2_s = mk.sbuf("cb2", [128, 16], F32)
    bdt_s = mk.sbuf("bdt", [128, 16], F32)
    dpc_s = mk.sbuf("dpc", [128, 16], F32)
    negA = [mk.sbuf("negA", [128, 16], F32) for _ in range(16)]
    grep = mk.sbuf("grep", [128, 2], F32)
    gcol = mk.sbuf("gcol", [2, 1], F32)
    g1r = mk.sbuf("g1r", [1, 2], F32)
    en16 = mk.sbuf("en16", [16, 16 * 128], BF16)
    with nc.Block() as b:
        @b.sync
        def _(e):
            for j in range(16):
                e.dma_start(out=cw1_s[:, 4 * j:4 * j + 4], in_=cw1[j * 128:(j + 1) * 128, :]).then_inc(s.h, 16)
                e.dma_start(out=cw2_s[:, 4 * j:4 * j + 4], in_=cw2[j * 128:(j + 1) * 128, :]).then_inc(s.h, 16)
                e.dma_start(out=negA[j][:, :], in_=A_log[j * 128:(j + 1) * 128, :]).then_inc(s.h, 16)
                s.inc(48)
            e.dma_start(out=gcol[:, :], in_=cg[:].rearrange("(p o) -> p o", o=1)).then_inc(s.h, 16)
            s.inc(16)
            e.wait_ge(s.h, s.v)
    load_param_cols(mk, cb1, cb1_s, kt=16)
    load_param_cols(mk, cb2, cb2_s, kt=16)
    load_param_cols(mk, b_dt, bdt_s, kt=16)
    load_param_cols(mk, D_param, dpc_s, kt=16)
    en16_np = np.zeros((16, 16 * 128), dtype=np.float32)
    for n in range(16):
        en16_np[n, n * 128:(n + 1) * 128] = 1.0
    en16_d = mk.const("en16", en16_np.astype(bfdt))
    with nc.Block() as b:
        @b.sync
        def _(e):
            e.dma_start(out=en16[:, :], in_=en16_d[:, :]).then_inc(s.h, 16)
            e.wait_ge(s.h, s.inc(16))
    with nc.Block() as b:
        @b.scalar
        def _(e):
            for j in range(16):
                e.activation(out=negA[j][:, :], in_=negA[j][:, :], func=AF.Exp)
            e.drain()
    with nc.Block() as b:
        @b.vector
        def _(e):
            for j in range(16):
                e.tensor_scalar_mul(out=negA[j][:, :], in0=negA[j][:, :], scalar1=-1.0)
            e.drain()
    with ExitStack() as lg:
        ps_g = lg.enter_context(nc.psum_tensor(mk.name("psg"), [128, 2], F32))
        ps_g1 = lg.enter_context(nc.psum_tensor(mk.name("psg1"), [1, 2], F32))
        with nc.Block() as b:
            @b.scalar
            def _(e):
                e.activation(out=gcol[:, :], in_=gcol[:, :], func=AF.Exp)
                e.drain()
        with nc.Block() as b:
            @b.vector
            def _(e):
                pass
            @b.tensor
            def _(e):
                e.transpose(ps_g1[0:1, 0:2], gcol[0:2, 0:1], mk.identf_s[0:2, 0:2])
        with nc.Block() as b:
            @b.vector
            def _(e):
                e.tensor_copy(out=g1r[0:1, :], in_=ps_g1[0:1, 0:2])
                e.drain()
                e.reduce_sum(out=g1r[0:1, 0:1].rearrange("a b -> a b") if False else gcol[0:1, 0:1],
                             in_=g1r[0:1, :], axis=AX.X)
                e.drain()
                e.reciprocal(out=gcol[0:1, 0:1], in_=gcol[0:1, 0:1])
                e.drain()
                e.tensor_scalar(out=g1r[0:1, :], in0=g1r[0:1, :], scalar1=gcol[0:1, 0:1],
                                scalar2=None, op0=ALU.mult)
                e.drain()
        with nc.Block() as b:
            @b.tensor
            def _(e):
                e.matmul(ps_g[:, :], mk.ones1f_s[0:1, :], g1r[0:1, :], start=True, stop=True)
        with nc.Block() as b:
            @b.vector
            def _(e):
                e.tensor_copy(out=grep[:, :], in_=ps_g[:, :])

    with ExitStack() as main:
        msb = lambda nm, shp, dt: main.enter_context(nc.sbuf_tensor(mk.name(nm), shp, dt))
        hcar = msb("hcar", [128, 256], F32)
        sumd = msb("sumd", [128, 16], F32)
        with nc.Block() as b:
            @b.vector
            def _(e):
                e.memset(hcar[:, :], 0.0)
                e.memset(sumd[:, :], 0.0)
        if mode == "B":
            with ExitStack() as lese:
                Eb = lese.enter_context(nc.sbuf_tensor(mk.name("eb3"), [128, 256], F32))
                Pb = lese.enter_context(nc.sbuf_tensor(mk.name("pb3"), [128, 256], F32))
                for j in (2, 1, 0):
                    with nc.Block() as b:
                        @b.sync
                        def _(e):
                            e.dma_start(out=Eb[:, :], in_=E_in[j][:, :]).then_inc(s.h, 16)
                            e.dma_start(out=Pb[:, :], in_=P_in[j][:, :]).then_inc(s.h, 16)
                            e.wait_ge(s.h, s.inc(32))
                    with nc.Block() as b:
                        @b.vector
                        def _(e):
                            e.tensor_mul(out=hcar[:, :], in0=hcar[:, :], in1=Pb[:, :])
                            e.drain()
                            e.tensor_add(out=hcar[:, :], in0=hcar[:, :], in1=Eb[:, :])
                            e.drain()
        z_t = [msb("z3", [128, 512], BF16) for _ in range(16)]
        pre_t = [msb("pre3", [128, 520], BF16) for _ in range(16)]
        xc_t = [pre_t[j][:, 8:520] for j in range(16)]
        y3 = msb("y3", [128, 512], F32)
        dtlow = msb("dtlow", [64, 512], BF16)
        Brow = msb("Brow", [16, 512], BF16)
        Crow = msb("Crow", [16, 512], BF16)
        asc = msb("asc", [128, 512], F32)
        bsc = msb("bsc", [128, 512], F32)
        hsc = msb("hsc", [128, 512], F32)
        cac = msb("cac3", [128, 512], F32)
        c2t = msb("c2t", [128, 512], F32)
        delt = c2t

        for mc in range(4):
            sl = slice(mc * 512, mc * 512 + 512)
            def xt_c(k, sl=sl):
                return xn_t[k][:, sl]
            def ev_xz(m, nci, nw, ps_ap, e):
                if m < 16:
                    e.copy(out=pre_t[m][:, 8:8 + nw], in_=ps_ap)
                else:
                    e.copy(out=z_t[m - 16][:, 0:nw], in_=ps_ap)
            gemm(mk, WinT, 1024, 4096, xt_c, [("scalar", ev_xz)], n_total=512, mgroup=1)
            if mc == 0:
                def xt_h(k):
                    return xnh_t[k][:, 0:6]
                def ev_hh(m, nci, nw, ps_ap, e):
                    e.copy(out=pre_t[m][:, 2:2 + nw], in_=ps_ap)
                gemm(mk, WinT, 1024, 4096, xt_h, [("scalar", ev_hh)], n_total=6, mgroup=1,
                     m_stop=16)
            for j in range(16):
                with nc.Block() as b:
                    @b.vector
                    def _(e):
                        e.tensor_scalar(out=cac[:, :], in0=pre_t[j][:, 5:5 + 512],
                                        scalar1=cw1_s[:, 4 * j:4 * j + 1], scalar2=None, op0=ALU.mult)
                        e.drain()
                        for i in range(1, 4):
                            e.scalar_tensor_tensor(out=cac[:, :], in0=pre_t[j][:, 5 + i:5 + i + 512],
                                                   scalar=cw1_s[:, 4 * j + i:4 * j + i + 1],
                                                   in1=cac[:, :], op0=ALU.mult, op1=ALU.add)
                            e.drain()
                        e.tensor_scalar(out=c2t[:, :], in0=pre_t[j][:, 2:2 + 512],
                                        scalar1=cw2_s[:, 4 * j:4 * j + 1], scalar2=None, op0=ALU.mult)
                        e.drain()
                        for i in range(1, 4):
                            e.scalar_tensor_tensor(out=c2t[:, :], in0=pre_t[j][:, 2 + 2 * i:2 + 2 * i + 512],
                                                   scalar=cw2_s[:, 4 * j + i:4 * j + i + 1],
                                                   in1=c2t[:, :], op0=ALU.mult, op1=ALU.add)
                            e.drain()
                        e.tensor_copy(out=pre_t[j][:, 2:8], in_=pre_t[j][:, 514:520])
                        e.drain()
                with nc.Block() as b:
                    @b.scalar
                    def _(e):
                        e.activation(out=cac[:, :], in_=cac[:, :], func=AF.Silu, bias=cb1_s[:, j:j + 1])
                        e.drain()
                        e.activation(out=c2t[:, :], in_=c2t[:, :], func=AF.Silu, bias=cb2_s[:, j:j + 1])
                        e.drain()
                with nc.Block() as b:
                    @b.vector
                    def _(e):
                        e.tensor_scalar(out=cac[:, :], in0=cac[:, :], scalar1=grep[:, 0:1],
                                        scalar2=None, op0=ALU.mult)
                        e.drain()
                        e.scalar_tensor_tensor(out=xc_t[j], in0=c2t[:, :], scalar=grep[:, 1:2],
                                               in1=cac[:, :], op0=ALU.mult, op1=ALU.add)
                        e.drain()
            def ev_xa(m, nci, nw, ps_ap, e):
                e.tensor_copy(out=dtlow[0:64, 0:nw], in_=ps_ap[0:64, 0:nw])
                e.drain()
                e.tensor_copy(out=Brow[0:16, 0:nw], in_=ps_ap[64:80, 0:nw])
                e.drain()
            gemm(mk, WxTa, 2048, 80, lambda k: xc_t[k], [("vector", ev_xa)], n_total=512, mgroup=1)
            def ev_xc(m, nci, nw, ps_ap, e):
                e.tensor_copy(out=Crow[0:16, 0:nw], in_=ps_ap[0:16, 0:nw])
                e.drain()
            gemm(mk, WxTc, 2048, 16, lambda k: xc_t[k], [("vector", ev_xc)], n_total=512, mgroup=1)
            with ExitStack() as lsc:
                ps_br = lsc.enter_context(nc.psum_tensor(mk.name("psbr"), [128, 512], F32))
                ps_cr = lsc.enter_context(nc.psum_tensor(mk.name("pscr"), [128, 512], F32))
                for j in range(16):
                    def ev_dt(m, nci, nw, ps_ap, e):
                        e.activation(out=delt[:, 0:nw], in_=ps_ap, func=AF.Identity,
                                     bias=bdt_s[:, j:j + 1])
                    gemm(mk, WdT, 64, 2048, lambda k: dtlow[0:64, :], [("scalar", ev_dt)],
                         n_total=512, mgroup=1, m_start=j, m_stop=j + 1)
                    with nc.Block() as b:
                        @b.scalar
                        def _(e):
                            e.activation(out=cac[:, :], in_=delt[:, :], func=AF.Abs)
                            e.drain()
                            e.activation(out=cac[:, :], in_=cac[:, :], func=AF.Exp, scale=-1.0)
                            e.drain()
                            e.activation(out=cac[:, :], in_=cac[:, :], func=AF.Ln, bias=1.0)
                            e.drain()
                    with nc.Block() as b:
                        @b.vector
                        def _(e):
                            e.tensor_scalar_max(out=delt[:, :], in0=delt[:, :], scalar1=0.0)
                            e.drain()
                            e.tensor_add(out=delt[:, :], in0=delt[:, :], in1=cac[:, :])
                            e.drain()
                            e.reduce_sum(out=cac[:, 0:1], in_=delt[:, :], axis=AX.X)
                            e.drain()
                            e.tensor_add(out=sumd[:, j:j + 1], in0=sumd[:, j:j + 1], in1=cac[:, 0:1])
                            e.drain()
                            e.tensor_mul(out=cac[:, :], in0=delt[:, :], in1=xc_t[j])
                            e.drain()
                    for n in range(16):
                        cidx = j * 16 + n
                        with nc.Block() as b:
                            @b.tensor
                            def _(e):
                                e.matmul(ps_br[:, :], en16[:, n * 128:(n + 1) * 128], Brow[0:16, :],
                                         start=True, stop=True)
                                if mode == "B":
                                    e.matmul(ps_cr[:, :], en16[:, n * 128:(n + 1) * 128], Crow[0:16, :],
                                             start=True, stop=True)
                        with nc.Block() as b:
                            @b.vector
                            def _(e):
                                e.tensor_scalar(out=asc[:, :], in0=delt[:, :],
                                                scalar1=negA[j][:, n:n + 1], scalar2=None, op0=ALU.mult)
                                e.drain()
                                e.tensor_mul(out=bsc[:, :], in0=cac[:, :], in1=ps_br[:, :])
                                e.drain()
                        with nc.Block() as b:
                            @b.scalar
                            def _(e):
                                e.activation(out=asc[:, :], in_=asc[:, :], func=AF.Exp)
                                e.drain()
                        with nc.Block() as b:
                            @b.vector
                            def _(e):
                                e.tensor_tensor_scan(out=hsc[:, :], data0=asc[:, :], data1=bsc[:, :],
                                                     initial=hcar[:, cidx:cidx + 1],
                                                     op0=ALU.mult, op1=ALU.add)
                                e.drain()
                                e.tensor_copy(out=hcar[:, cidx:cidx + 1], in_=hsc[:, 511:512])
                                e.drain()
                                if mode == "B":
                                    e.tensor_mul(out=hsc[:, :], in0=hsc[:, :], in1=ps_cr[:, :])
                                    e.drain()
                                    if n == 0:
                                        e.tensor_copy(out=y3[:, :], in_=hsc[:, :])
                                    else:
                                        e.tensor_add(out=y3[:, :], in0=y3[:, :], in1=hsc[:, :])
                                    e.drain()
                    if mode == "B":
                        with nc.Block() as b:
                            @b.vector
                            def _(e):
                                e.scalar_tensor_tensor(out=y3[:, :], in0=xc_t[j],
                                                       scalar=dpc_s[:, j:j + 1], in1=y3[:, :],
                                                       op0=ALU.mult, op1=ALU.add)
                                e.drain()
                        with nc.Block() as b:
                            @b.scalar
                            def _(e):
                                e.activation(out=cac[:, :], in_=z_t[j][:, :], func=AF.Silu)
                                e.drain()
                        with nc.Block() as b:
                            @b.vector
                            def _(e):
                                e.tensor_mul(out=y3[:, :], in0=y3[:, :], in1=cac[:, :])
                                e.drain()
                                e.tensor_copy(out=xc_t[j], in_=y3[:, :])
                                e.drain()
            if mode == "B":
                def ev_s(m, nci, nw, ps_ap, e, sl=sl):
                    e.copy(out=s_t[m][:, sl.start:sl.start + nw], in_=ps_ap)
                gemm(mk, WoT, 2048, 1024, lambda k: xc_t[k], [("scalar", ev_s)],
                     n_total=512, mgroup=1)
        if mode == "A":
            Pm = msb("Pm", [128, 256], F32)
            with nc.Block() as b:
                @b.vector
                def _(e):
                    for j in range(16):
                        e.tensor_scalar(out=Pm[:, j * 16:(j + 1) * 16], in0=negA[j][:, :],
                                        scalar1=sumd[:, j:j + 1], scalar2=None, op0=ALU.mult)
                    e.drain()
            with nc.Block() as b:
                @b.scalar
                def _(e):
                    e.activation(out=Pm[:, :], in_=Pm[:, :], func=AF.Exp)
                    e.drain()
            with nc.Block() as b:
                @b.sync
                def _(e):
                    e.dma_start(out=E_o[:, :], in_=hcar[:, :]).then_inc(s.h, 16)
                    e.dma_start(out=P_o[:, :], in_=Pm[:, :]).then_inc(s.h, 16)
                    e.wait_ge(s.h, s.inc(32))


def build_l3a():
    mk = MK()
    x_in = mk.din("x", (D, T))
    counts_o = mk.dout("counts", (1, 8))
    setup_common(mk)
    x_t = [mk.sbuf("x", [128, T], F32) for _ in range(8)]
    xn_t = [mk.sbuf("xn", [128, T], BF16) for _ in range(8)]
    load_x(mk, x_in, x_t)
    layernorm(mk, x_t, xn_t)
    emit_l3_attn(mk, x_t, xn_t, "A")
    emit_l3_counts(mk, xn_t, counts_o)
    return mk.finish()


def emit_l3_counts(mk, xn_t, counts_o):
    nc = mk.nc
    s = mk.sem("dma")
    rw = mk.din("router_w", (1024, 8))
    rb = mk.din("router_b", (8,))
    Rw = prep_w(mk, rw, "io", 1024, 8, tag="l3rw")
    with ExitStack() as les:
        sb = lambda nm, shp, dt: les.enter_context(nc.sbuf_tensor(mk.name(nm), shp, dt))
        lg_fm = sb("lgfm3", [8, T], F32)
        rb_s = sb("rbs3", [1, 8], F32)
        rbrep = sb("rbrep3", [128, 8], F32)
        oh = sb("oh3", [128, 8], F32)
        oh2 = sb("oh23", [128, 8], F32)
        ohb = sb("ohb3", [128, 8], BF16)
        csum = sb("csum3", [128, 8], F32)
        onesf = sb("onesf3", [128, 8], F32)
        cv = sb("cv3", [128, 2], F32)
        cnt_s = sb("cnts3", [1, 8], F32)
        onescol_b = sb("oncb3", [128, 1], BF16)
        ps_t = les.enter_context(nc.psum_tensor(mk.name("pst3"), [128, 8], F32))
        ps_c = les.enter_context(nc.psum_tensor(mk.name("psc3"), [1, 8], F32))
        ps_r = les.enter_context(nc.psum_tensor(mk.name("psr3"), [128, 8], F32))
        def ev_lg(m, nci, nw, ps_ap, e):
            e.tensor_copy(out=lg_fm[0:8, nci * 512:nci * 512 + nw], in_=ps_ap)
        gemm(mk, Rw, 1024, 8, lambda k: xn_t[k], [("vector", ev_lg)], mgroup=1)
        with nc.Block() as b:
            @b.sync
            def _(e):
                e.dma_start(out=rb_s[:, :], in_=rb[:].rearrange("(o e) -> o e", o=1)).then_inc(s.h, 16)
                e.wait_ge(s.h, s.inc(16))
            @b.vector
            def _(e):
                e.memset(onesf[:, :], 1.0)
                e.memset(onescol_b[:, :], 1.0)
        with nc.Block() as b:
            @b.tensor
            def _(e):
                e.matmul(ps_r[:, :], mk.ones1f_s[0:1, :], rb_s[0:1, :], start=True, stop=True)
        with nc.Block() as b:
            @b.vector
            def _(e):
                e.tensor_copy(out=rbrep[:, :], in_=ps_r[:, :])
        for tt in range(16):
            with nc.Block() as b:
                @b.tensor
                def _(e):
                    e.transpose(ps_t[:, 0:8], lg_fm[0:8, tt * 128:(tt + 1) * 128],
                                mk.identf_s[0:8, 0:8])
            with nc.Block() as b:
                @b.vector
                def _(e):
                    def Dr():
                        e.drain()
                    e.tensor_copy(out=csum[:, :], in_=ps_t[:, 0:8])
                    Dr()
                    e.tensor_add(out=csum[:, :], in0=csum[:, :], in1=rbrep[:, :])
                    Dr()
                    e.reduce_max(out=cv[:, 0:1], in_=csum[:, :], axis=AX.X)
                    Dr()
                    e.tensor_scalar(out=oh[:, :], in0=csum[:, :], scalar1=cv[:, 0:1],
                                    scalar2=None, op0=ALU.is_ge)
                    Dr()
                    e.tensor_tensor_scan(out=oh2[:, :], data0=onesf[:, :], data1=oh[:, :],
                                         initial=0.0, op0=ALU.mult, op1=ALU.add)
                    Dr()
                    e.tensor_sub(out=oh2[:, :], in0=oh2[:, :], in1=oh[:, :])
                    Dr()
                    e.tensor_scalar(out=oh2[:, :], in0=oh2[:, :], scalar1=0.5,
                                    scalar2=None, op0=ALU.is_lt)
                    Dr()
                    e.tensor_mul(out=oh[:, :], in0=oh[:, :], in1=oh2[:, :])
                    Dr()
                    e.tensor_scalar(out=oh2[:, :], in0=oh[:, :], scalar1=1e9,
                                    scalar2=None, op0=ALU.mult)
                    Dr()
                    e.tensor_sub(out=csum[:, :], in0=csum[:, :], in1=oh2[:, :])
                    Dr()
                    e.reduce_max(out=cv[:, 1:2], in_=csum[:, :], axis=AX.X)
                    Dr()
                    e.tensor_scalar(out=oh2[:, :], in0=csum[:, :], scalar1=cv[:, 1:2],
                                    scalar2=None, op0=ALU.is_ge)
                    Dr()
                    e.tensor_tensor_scan(out=csum[:, :], data0=onesf[:, :], data1=oh2[:, :],
                                         initial=0.0, op0=ALU.mult, op1=ALU.add)
                    Dr()
                    e.tensor_sub(out=csum[:, :], in0=csum[:, :], in1=oh2[:, :])
                    Dr()
                    e.tensor_scalar(out=csum[:, :], in0=csum[:, :], scalar1=0.5,
                                    scalar2=None, op0=ALU.is_lt)
                    Dr()
                    e.tensor_mul(out=oh2[:, :], in0=oh2[:, :], in1=csum[:, :])
                    Dr()
                    e.tensor_add(out=oh[:, :], in0=oh[:, :], in1=oh2[:, :])
                    Dr()
                    e.tensor_copy(out=ohb[:, :], in_=oh[:, :])
                    Dr()
            with nc.Block() as b:
                @b.tensor
                def _(e):
                    e.matmul(ps_c[0:1, 0:8], onescol_b[:, 0:1], ohb[:, :],
                             start=(tt == 0), stop=(tt == 15))
        with nc.Block() as b:
            @b.vector
            def _(e):
                e.tensor_copy(out=cnt_s[0:1, :], in_=ps_c[0:1, 0:8])
        with nc.Block() as b:
            @b.sync
            def _(e):
                e.dma_start(out=counts_o[:, :], in_=cnt_s[0:1, :]).then_inc(s.h, 16)
                e.wait_ge(s.h, s.inc(16))


def emit_l3_ffn(mk, x_t, xn_t, s_t):
    nc = mk.nc
    s = mk.sem("dma")
    CAP = 5120
    rw = mk.din("router_w", (1024, 8))
    rb = mk.din("router_b", (8,))
    w1 = mk.din("w1", (8, 1024, 2048))
    b1 = mk.din("b1", (8, 2048))
    w2 = mk.din("w2", (8, 2048, 1024))
    b2 = mk.din("b2", (8, 1024))
    counts_all = mk.din("counts_all", (8, 8))
    maskrow = mk.din("maskrow", (8, 1))
    Rw = prep_w(mk, rw, "io", 1024, 8, tag="l3frw")
    W1 = [prep_w(mk, w1[e], "io", 1024, 2048, tag=f"l3w1_{e}") for e in range(8)]
    W2 = [prep_w(mk, w2[e], "io", 2048, 1024, tag=f"l3w2_{e}") for e in range(8)]
    with ExitStack() as les:
        sb = lambda nm, shp, dt: les.enter_context(nc.sbuf_tensor(mk.name(nm), shp, dt))
        lg_fm = sb("lgfm", [8, T], F32)
        cnt_s = sb("cnta", [8, 8], F32)
        mrow = sb("mrow", [8, 1], F32)
        prefix = sb("prefix", [1, 8], F32)
        comb_fm = sb("combfm", [8, T], BF16)
        rb_s = sb("rbs", [1, 8], F32)
        rbrep = sb("rbrep", [128, 8], F32)
        oh1 = sb("oh1", [128, 8], F32)
        oh2 = sb("oh2", [128, 8], F32)
        ohb = sb("ohb", [128, 8], BF16)
        csum = sb("csum", [128, 8], F32)
        onesf = sb("onesf", [128, 8], F32)
        cv = sb("cvl", [128, 6], F32)
        trib = sb("trib", [128, 128], BF16)
        acc = [sb("acc", [128, 512], F32) for _ in range(8)]
        he = [sb("he", [128, 512], BF16) for _ in range(16)]
        comb_rep = sb("combrep", [128, 512], BF16)
        b1c = sb("b1c", [128, 16], F32)
        b2c = sb("b2c", [128, 8], F32)
        tmpf = sb("tmpfb", [128, 512], F32)
        gtile = sb("gtileb", [128, 512], F32)
        def ev_lg(m, nci, nw, ps_ap, e):
            e.tensor_copy(out=lg_fm[0:8, nci * 512:nci * 512 + nw], in_=ps_ap)
        gemm(mk, Rw, 1024, 8, lambda k: xn_t[k], [("vector", ev_lg)], mgroup=1)
        ps_t = les.enter_context(nc.psum_tensor(mk.name("pstb"), [128, 8], F32))
        ps_rk = les.enter_context(nc.psum_tensor(mk.name("psrk"), [128, 8], F32))
        ps_pr = les.enter_context(nc.psum_tensor(mk.name("pspr"), [128, 8], F32))
        ps_ct = les.enter_context(nc.psum_tensor(mk.name("psct"), [1, 8], F32))
        ps_cm = les.enter_context(nc.psum_tensor(mk.name("pscmb"), [8, 512], F32))
        ps_rep2 = les.enter_context(nc.psum_tensor(mk.name("prep2b"), [128, 512], F32))
        tri = mk.const("tril_strict", (np.arange(128)[:, None] < np.arange(128)[None, :])
                       .astype(np.float32).astype(bfdt))
        rsel_s = sb("rsels", [8, 8 * 128], BF16)
        onescol_b = sb("onescb2", [128, 1], BF16)
        with nc.Block() as b:
            @b.sync
            def _(e):
                e.dma_start(out=trib[:, :], in_=tri[:, :]).then_inc(s.h, 16)
                e.dma_start(out=cnt_s[:, :], in_=counts_all[:, :]).then_inc(s.h, 16)
                e.dma_start(out=mrow[:, :], in_=maskrow[:, :]).then_inc(s.h, 16)
                e.dma_start(out=rb_s[:, :], in_=rb[:].rearrange("(o e) -> o e", o=1)).then_inc(s.h, 16)
                for e_ in range(8):
                    e.dma_start(out=rsel_s[:, e_ * 128:(e_ + 1) * 128],
                                in_=mk.rowsel_d(e_)[:, :]).then_inc(s.h, 16)
                    s.inc(16)
                e.wait_ge(s.h, s.inc(64))
            @b.vector
            def _(e):
                e.memset(onesf[:, :], 1.0)
                e.memset(onescol_b[:, :], 1.0)
        with nc.Block() as b:
            @b.vector
            def _(e):
                e.tensor_scalar(out=cnt_s[:, :], in0=cnt_s[:, :], scalar1=mrow[0:8, 0:1],
                                scalar2=None, op0=ALU.mult)
                e.drain()
        with nc.Block() as b:
            @b.tensor
            def _(e):
                e.matmul(ps_ct[0:1, 0:8], mk.onescol_s[0:8, 0:1], cnt_s[:, :], start=True, stop=True)
                e.matmul(ps_pr[:, 0:8], mk.ones1f_s[0:1, :], rb_s[0:1, :], start=True, stop=True)
        with nc.Block() as b:
            @b.vector
            def _(e):
                e.tensor_copy(out=prefix[0:1, :], in_=ps_ct[0:1, 0:8])
                e.tensor_copy(out=rbrep[:, :], in_=ps_pr[:, 0:8])
        for tt in range(16):
            with nc.Block() as b:
                @b.tensor
                def _(e):
                    e.transpose(ps_t[:, 0:8], lg_fm[0:8, tt * 128:(tt + 1) * 128],
                                mk.identf_s[0:8, 0:8])
            with nc.Block() as b:
                @b.vector
                def _(e):
                    e.tensor_copy(out=csum[:, :], in_=ps_t[:, 0:8])
                    e.drain()
                    e.tensor_add(out=csum[:, :], in0=csum[:, :], in1=rbrep[:, :])
                    e.drain()
                    e.reduce_max(out=cv[:, 0:1], in_=csum[:, :], axis=AX.X, negate=True)
                    e.drain()
            with nc.Block() as b:
                @b.scalar
                def _(e):
                    e.activation(out=csum[:, :], in_=csum[:, :], func=AF.Exp, bias=cv[:, 0:1])
                    e.drain()
            with nc.Block() as b:
                @b.vector
                def _(e):
                    def Dr():
                        e.drain()
                    e.reduce_sum(out=cv[:, 1:2], in_=csum[:, :], axis=AX.X)
                    Dr()
                    e.reciprocal(out=cv[:, 1:2], in_=cv[:, 1:2])
                    Dr()
                    e.tensor_scalar(out=csum[:, :], in0=csum[:, :], scalar1=cv[:, 1:2],
                                    scalar2=None, op0=ALU.mult)
                    Dr()
                    e.reduce_max(out=cv[:, 2:3], in_=csum[:, :], axis=AX.X)
                    Dr()
                    e.tensor_scalar(out=oh1[:, :], in0=csum[:, :], scalar1=cv[:, 2:3],
                                    scalar2=None, op0=ALU.is_ge)
                    Dr()
                    e.tensor_tensor_scan(out=oh2[:, :], data0=onesf[:, :], data1=oh1[:, :],
                                         initial=0.0, op0=ALU.mult, op1=ALU.add)
                    Dr()
                    e.tensor_sub(out=oh2[:, :], in0=oh2[:, :], in1=oh1[:, :])
                    Dr()
                    e.tensor_scalar(out=oh2[:, :], in0=oh2[:, :], scalar1=0.5,
                                    scalar2=None, op0=ALU.is_lt)
                    Dr()
                    e.tensor_mul(out=oh1[:, :], in0=oh1[:, :], in1=oh2[:, :])
                    Dr()
                    e.tensor_scalar(out=oh2[:, :], in0=oh1[:, :], scalar1=1e9,
                                    scalar2=None, op0=ALU.mult)
                    Dr()
                    e.tensor_sub(out=gtile[:, 0:8], in0=csum[:, :], in1=oh2[:, :])
                    Dr()
                    e.reduce_max(out=cv[:, 3:4], in_=gtile[:, 0:8], axis=AX.X)
                    Dr()
                    e.tensor_scalar(out=oh2[:, :], in0=gtile[:, 0:8], scalar1=cv[:, 3:4],
                                    scalar2=None, op0=ALU.is_ge)
                    Dr()
                    e.tensor_tensor_scan(out=gtile[:, 0:8], data0=onesf[:, :], data1=oh2[:, :],
                                         initial=0.0, op0=ALU.mult, op1=ALU.add)
                    Dr()
                    e.tensor_sub(out=gtile[:, 0:8], in0=gtile[:, 0:8], in1=oh2[:, :])
                    Dr()
                    e.tensor_scalar(out=gtile[:, 0:8], in0=gtile[:, 0:8], scalar1=0.5,
                                    scalar2=None, op0=ALU.is_lt)
                    Dr()
                    e.tensor_mul(out=oh2[:, :], in0=oh2[:, :], in1=gtile[:, 0:8])
                    Dr()
                    e.tensor_mul(out=gtile[:, 0:8], in0=oh1[:, :], in1=csum[:, :])
                    Dr()
                    e.reduce_sum(out=cv[:, 2:3], in_=gtile[:, 0:8], axis=AX.X)
                    Dr()
                    e.tensor_mul(out=gtile[:, 0:8], in0=oh2[:, :], in1=csum[:, :])
                    Dr()
                    e.reduce_sum(out=cv[:, 3:4], in_=gtile[:, 0:8], axis=AX.X)
                    Dr()
                    e.tensor_add(out=cv[:, 4:5], in0=cv[:, 2:3], in1=cv[:, 3:4])
                    Dr()
                    e.tensor_scalar_max(out=cv[:, 4:5], in0=cv[:, 4:5], scalar1=1e-9)
                    Dr()
                    e.reciprocal(out=cv[:, 4:5], in_=cv[:, 4:5])
                    Dr()
                    e.tensor_mul(out=cv[:, 2:3], in0=cv[:, 2:3], in1=cv[:, 4:5])
                    Dr()
                    e.tensor_mul(out=cv[:, 3:4], in0=cv[:, 3:4], in1=cv[:, 4:5])
                    Dr()
                    e.tensor_add(out=csum[:, :], in0=oh1[:, :], in1=oh2[:, :])
                    Dr()
                    e.tensor_copy(out=ohb[:, :], in_=csum[:, :])
                    Dr()
            with nc.Block() as b:
                @b.tensor
                def _(e):
                    e.matmul(ps_rk[:, 0:8], trib[:, :], ohb[:, :], start=True, stop=True)
                    e.matmul(ps_ct[0:1, 0:8], onescol_b[:, 0:1], ohb[:, :], start=True, stop=True)
                    e.matmul(ps_pr[:, 0:8], mk.ones1f_s[0:1, :], prefix[0:1, :], start=True, stop=True)
            with nc.Block() as b:
                @b.vector
                def _(e):
                    def Dr():
                        e.drain()
                    e.tensor_copy(out=csum[:, :], in_=ps_pr[:, 0:8])
                    Dr()
                    e.tensor_add(out=csum[:, :], in0=csum[:, :], in1=ps_rk[:, 0:8])
                    Dr()
                    e.tensor_scalar(out=csum[:, :], in0=csum[:, :], scalar1=float(CAP) - 0.5,
                                    scalar2=None, op0=ALU.is_lt)
                    Dr()
                    e.tensor_scalar(out=oh1[:, :], in0=oh1[:, :], scalar1=cv[:, 2:3],
                                    scalar2=None, op0=ALU.mult)
                    Dr()
                    e.tensor_scalar(out=oh2[:, :], in0=oh2[:, :], scalar1=cv[:, 3:4],
                                    scalar2=None, op0=ALU.mult)
                    Dr()
                    e.tensor_add(out=oh1[:, :], in0=oh1[:, :], in1=oh2[:, :])
                    Dr()
                    e.tensor_mul(out=oh1[:, :], in0=oh1[:, :], in1=csum[:, :])
                    Dr()
                    e.tensor_add(out=prefix[0:1, :], in0=prefix[0:1, :], in1=ps_ct[0:1, 0:8])
                    Dr()
            with nc.Block() as b:
                @b.tensor
                def _(e):
                    e.transpose(ps_cm[0:8, 0:128], oh1[:, :], mk.identf_s[:, :])
            with nc.Block() as b:
                @b.vector
                def _(e):
                    e.tensor_copy(out=comb_fm[0:8, tt * 128:(tt + 1) * 128], in_=ps_cm[0:8, 0:128])
        for nci in range(4):
            sl = slice(nci * 512, nci * 512 + 512)
            def xt_c(k, sl=sl):
                return xn_t[k][:, sl]
            with nc.Block() as b:
                @b.vector
                def _(e):
                    for m in range(8):
                        e.memset(acc[m][:, :], 0.0)
            for e_ in range(8):
                load_param_cols(mk, b1[e_], b1c, kt=16)
                load_param_cols(mk, b2[e_], b2c, kt=8)
                with nc.Block() as b:
                    @b.tensor
                    def _(e):
                        e.matmul(ps_rep2[:, :], rsel_s[:, e_ * 128:(e_ + 1) * 128],
                                 comb_fm[0:8, sl], start=True, stop=True)
                with nc.Block() as b:
                    @b.vector
                    def _(e):
                        e.tensor_copy(out=comb_rep[:, :], in_=ps_rep2[:, :])
                def ev_he(m, nci2, nw, ps_ap, e):
                    e.activation(out=he[m][:, 0:nw], in_=ps_ap, func=AF.Gelu,
                                 bias=b1c[:, m:m + 1])
                gemm(mk, W1[e_], 1024, 2048, xt_c, [("scalar", ev_he)], n_total=512, mgroup=1)
                with nc.Block() as b:
                    @b.vector
                    def _(e):
                        for m in range(16):
                            e.tensor_mul(out=he[m][:, :], in0=he[m][:, :], in1=comb_rep[:, :])
                def ev_acc(m, nci2, nw, ps_ap, e):
                    e.tensor_scalar(out=tmpf[:, 0:nw], in0=comb_rep[:, 0:nw], scalar1=b2c[:, m:m + 1],
                                    scalar2=None, op0=ALU.mult)
                    e.drain()
                    e.tensor_add(out=tmpf[:, 0:nw], in0=tmpf[:, 0:nw], in1=ps_ap)
                    e.drain()
                    e.tensor_add(out=acc[m][:, 0:nw], in0=acc[m][:, 0:nw], in1=tmpf[:, 0:nw])
                    e.drain()
                gemm(mk, W2[e_], 2048, 1024, lambda k: he[k][:, :], [("vector", ev_acc)],
                     n_total=512, mgroup=1)
            with nc.Block() as b:
                @b.scalar
                def _(e):
                    for m in range(8):
                        e.activation(out=acc[m][:, :], in_=acc[m][:, :], func=AF.Silu)
            with nc.Block() as b:
                @b.vector
                def _(e):
                    for m in range(8):
                        e.tensor_copy(out=gtile[:, :], in_=s_t[m][:, sl])
                        e.drain()
                        e.tensor_mul(out=tmpf[:, :], in0=acc[m][:, :], in1=gtile[:, :])
                        e.drain()
                        e.tensor_add(out=x_t[m][:, sl], in0=x_t[m][:, sl], in1=tmpf[:, :])
                        e.drain()


def build_l3b():
    mk = MK()
    x_in = mk.din("x", (D, T))
    x_out = mk.dout("x_new", (D, T))
    setup_common(mk)
    x_t = [mk.sbuf("x", [128, T], F32) for _ in range(8)]
    xn_t = [mk.sbuf("xn", [128, T], BF16) for _ in range(8)]
    s_t = [mk.sbuf("s3", [128, T], BF16) for _ in range(8)]
    load_x(mk, x_in, x_t)
    layernorm(mk, x_t, xn_t)
    emit_l3_attn(mk, x_t, xn_t, "B", s_t=s_t)
    emit_l3_ffn(mk, x_t, xn_t, s_t)
    s = mk.sem("dma")
    nc = mk.nc
    with nc.Block() as b:
        @b.sync
        def _(e):
            for k in range(8):
                e.dma_start(out=x_out[k * 128:(k + 1) * 128, :], in_=x_t[k][:, :]).then_inc(s.h, 16)
                s.inc(16)
            e.wait_ge(s.h, s.v)
    return mk.finish()




from concourse.bass_utils import run_bass_kernel_spmd


def _shard_fm(x_full):
    out = []
    for c in range(8):
        bb, q = c // 4, (c % 4) * 2048
        out.append(np.ascontiguousarray(x_full[bb, q:q + 2048, :].T.astype(np.float32)))
    return out


def _unshard_fm(parts):
    x = np.zeros((2, 8192, 1024), np.float32)
    for c in range(8):
        bb, q = c // 4, (c % 4) * 2048
        x[bb, q:q + 2048, :] = np.asarray(parts[c]).T
    return x


def kernel(**inp):
    cores = list(range(8))
    x_sh = _shard_fm(inp["L0_attn_x"])
    Z3 = np.zeros((1024, 3), np.float32)
    Z6 = np.zeros((1024, 6), np.float32)

    nc = build_l0a()
    res = run_bass_kernel_spmd(nc, [dict(x=x_sh[c], Wqkv=inp["L0_attn_Wqkv"]) for c in cores],
                               core_ids=cores)
    kvp = [np.asarray(res.results[c]["kv_p"]) for c in cores]
    ksp = [np.asarray(res.results[c]["ksum_p"]) for c in cores]
    nc = build_l0b_full()
    in_maps = []
    for c in cores:
        g = range(4 * (c // 4), 4 * (c // 4) + 4)
        kv_all = np.ascontiguousarray(np.stack([kvp[j] for j in g], axis=1))
        ks_all = np.ascontiguousarray(np.stack([ksp[j] for j in g], axis=-1).transpose(1, 0, 2))
        in_maps.append(dict(
            x=x_sh[c], Wqkv=inp["L0_attn_Wqkv"], Wproj=inp["L0_attn_Wproj"],
            bproj=inp["L0_attn_bproj"], kv_all=kv_all, ks_all=ks_all,
            router_w=inp["L0_ffn_router_w"], router_bias=inp["L0_ffn_router_bias"],
            sw1=inp["L0_ffn_sw1"], sw3=inp["L0_ffn_sw3"], sw2=inp["L0_ffn_sw2"],
            ew1=inp["L0_ffn_ew1"], ew2=inp["L0_ffn_ew2"]))
    res = run_bass_kernel_spmd(nc, in_maps, core_ids=cores)
    x_sh = [np.asarray(res.results[c]["x_new"]) for c in cores]

    l1a_in = dict(W_in=inp["L1_attn_W_in"], b_in=inp["L1_attn_b_in"],
                  log_A=inp["L1_attn_log_A"], B_p=inp["L1_attn_B_p"],
                  router_w=inp["L1_ffn_router_w"])
    nc = build_l1a()
    res = run_bass_kernel_spmd(nc, [dict(x=x_sh[c], **l1a_in) for c in cores], core_ids=cores)
    states = [np.asarray(res.results[c]["state"]) for c in cores]
    counts_all = np.ascontiguousarray(
        np.stack([np.asarray(res.results[c]["counts"])[0] for c in cores]).astype(np.float32))
    nc = build_l1b()
    in_maps = []
    for c in cores:
        carry = states[c - 1] if c % 4 != 0 else np.zeros((128, 8), np.float32)
        in_maps.append(dict(
            x=x_sh[c], **l1a_in,
            C_p=inp["L1_attn_C_p"], W_out=inp["L1_attn_W_out"], b_out=inp["L1_attn_b_out"],
            gamma=inp["L1_attn_gamma"], beta=inp["L1_attn_beta"],
            w1=inp["L1_ffn_w1"], b1=inp["L1_ffn_b1"], w2=inp["L1_ffn_w2"], b2=inp["L1_ffn_b2"],
            carry=np.ascontiguousarray(carry), counts_all=counts_all,
            maskrow=(np.arange(8) < c).astype(np.float32).reshape(8, 1)))
    res = run_bass_kernel_spmd(nc, in_maps, core_ids=cores)
    x_sh = [np.asarray(res.results[c]["x_new"]) for c in cores]

    l2_attn = dict(W_in=inp["L2_attn_W_in"], conv_w=inp["L2_attn_conv_w"],
                   conv_b=inp["L2_attn_conv_b"], dt_bias=inp["L2_attn_dt_bias"],
                   A_log=inp["L2_attn_A_log"], D_par=inp["L2_attn_D"])
    nc = build_l2a()
    in_maps = []
    for c in cores:
        xh = np.ascontiguousarray(x_sh[c - 1][:, -3:]) if c % 4 != 0 else Z3
        in_maps.append(dict(x=x_sh[c], xh=xh, **l2_attn))
    res = run_bass_kernel_spmd(nc, in_maps, core_ids=cores)
    E2 = [np.asarray(res.results[c]["E_out"]) for c in cores]
    P2 = [np.asarray(res.results[c]["P_out"]) for c in cores]
    nc = build_l2b()
    ZE2 = np.zeros((128, 2048), np.float32)
    ZP2 = np.zeros((128, 32), np.float32)
    in_maps = []
    for c in cores:
        xh = np.ascontiguousarray(x_sh[c - 1][:, -3:]) if c % 4 != 0 else Z3
        m = dict(x=x_sh[c], xh=xh, **l2_attn,
                 norm_w=inp["L2_attn_norm_w"], W_out=inp["L2_attn_W_out"],
                 w1_w=inp["L2_ffn_w1_w"], w1_s=inp["L2_ffn_w1_s"], w2_w=inp["L2_ffn_w2_w"],
                 w2_s=inp["L2_ffn_w2_s"], h1_w=inp["L2_ffn_h1_w"], h1_s=inp["L2_ffn_h1_s"],
                 h2_w=inp["L2_ffn_h2_w"], h2_s=inp["L2_ffn_h2_s"], w3_w=inp["L2_ffn_w3_w"],
                 w3_s=inp["L2_ffn_w3_s"])
        for j in range(3):
            pc = c - 1 - j
            if pc // 4 == c // 4 and pc >= 0:
                m[f"E{j}"], m[f"P{j}"] = E2[pc], P2[pc]
            else:
                m[f"E{j}"], m[f"P{j}"] = ZE2, ZP2
        in_maps.append(m)
    res = run_bass_kernel_spmd(nc, in_maps, core_ids=cores)
    x_sh = [np.asarray(res.results[c]["x_new"]) for c in cores]

    l3_attn = dict(W_in=inp["L3_attn_W_in"], conv_w1=inp["L3_attn_conv_w1"],
                   conv_b1=inp["L3_attn_conv_b1"], conv_w2=inp["L3_attn_conv_w2"],
                   conv_b2=inp["L3_attn_conv_b2"], conv_gates=inp["L3_attn_conv_gates"],
                   W_x=inp["L3_attn_W_x"], W_dt=inp["L3_attn_W_dt"], b_dt=inp["L3_attn_b_dt"],
                   A_log=inp["L3_attn_A_log"], D_param=inp["L3_attn_D_param"])
    nc = build_l3a()
    in_maps = []
    for c in cores:
        xh = np.ascontiguousarray(x_sh[c - 1][:, -6:]) if c % 4 != 0 else Z6
        in_maps.append(dict(x=x_sh[c], xh=xh, router_w=inp["L3_ffn_router_w"],
                            router_b=inp["L3_ffn_router_b"], **l3_attn))
    res = run_bass_kernel_spmd(nc, in_maps, core_ids=cores)
    E3 = [np.asarray(res.results[c]["E_out"]) for c in cores]
    P3 = [np.asarray(res.results[c]["P_out"]) for c in cores]
    counts_all = np.ascontiguousarray(
        np.stack([np.asarray(res.results[c]["counts"])[0] for c in cores]).astype(np.float32))
    nc = build_l3b()
    ZE3 = np.zeros((128, 256), np.float32)
    in_maps = []
    for c in cores:
        xh = np.ascontiguousarray(x_sh[c - 1][:, -6:]) if c % 4 != 0 else Z6
        m = dict(x=x_sh[c], xh=xh, **l3_attn,
                 W_out=inp["L3_attn_W_out"], router_w=inp["L3_ffn_router_w"],
                 router_b=inp["L3_ffn_router_b"],
                 w1=inp["L3_ffn_w1"], b1=inp["L3_ffn_b1"], w2=inp["L3_ffn_w2"], b2=inp["L3_ffn_b2"],
                 counts_all=counts_all,
                 maskrow=(np.arange(8) < c).astype(np.float32).reshape(8, 1))
        for j in range(3):
            pc = c - 1 - j
            if pc // 4 == c // 4 and pc >= 0:
                m[f"E{j}"], m[f"P{j}"] = E3[pc], P3[pc]
            else:
                m[f"E{j}"], m[f"P{j}"] = ZE3, ZE3
        in_maps.append(m)
    res = run_bass_kernel_spmd(nc, in_maps, core_ids=cores)
    x_sh = [np.asarray(res.results[c]["x_new"]) for c in cores]

    out = np.zeros((2, 8192, 1024), np.float32)
    for c in cores:
        bb, q = c // 4, (c % 4) * 2048
        out[bb, q:q + 2048, :] = x_sh[c].T
    return out
